# version 05 (final)
# baseline (speedup 1.0000x reference)
import math
import numpy as np
import ml_dtypes
import concourse.bass as bass
import concourse.mybir as mybir
from concourse.bass_utils import run_bass_kernel_spmd

F32 = mybir.dt.float32
BF16 = mybir.dt.bfloat16
I32 = mybir.dt.int32
AF = mybir.ActivationFunctionType
ALU = mybir.AluOpType
AX = mybir.AxisListType

NCORES = 8
D = 1024
S = 4096
TC = 2048
B = 4
PB = 128
NKC = D // PB
BF = ml_dtypes.bfloat16
NEG = -1e9

_PROG_CACHE = {}
SIM_MODE = False



def pack_w(w):
    K, M = w.shape
    assert K % PB == 0 and M % PB == 0, (K, M)
    t = np.asarray(w, np.float32).reshape(K // PB, PB, M // PB, PB)
    return np.ascontiguousarray(t.transpose(2, 1, 0, 3))


def fmh(x):
    return np.ascontiguousarray(np.asarray(x).T)


def to_tm(x):
    return np.ascontiguousarray(np.asarray(x))



class SemT:
    def __init__(self, h):
        self.h = h
        self.n = 0


class Prog:
    def __init__(self, name="prog"):
        self.nc = bass.Bass("TRN2", target_bir_lowering=False)
        self.name = name
        self._ctxs = []
        self._sems = {}
        self.inputs = {}
        self.outputs = {}

    @property
    def te(self): return self.nc.tensor
    @property
    def ve(self): return self.nc.vector
    @property
    def se(self): return self.nc.scalar
    @property
    def ge(self): return self.nc.gpsimd
    @property
    def qe(self): return self.nc.sync

    def sem(self, name):
        if name not in self._sems:
            h = self._ctx(self.nc.semaphore(name))
            self._sems[name] = SemT(h)
        return self._sems[name]

    def _ctx(self, cm):
        v = cm.__enter__()
        self._ctxs.append(cm)
        return v

    def din(self, name, shape, dtype=F32):
        ap = self.nc.dram_tensor(name, list(shape), dtype, kind="ExternalInput").ap()
        self.inputs[name] = dtype
        return ap

    def dout(self, name, shape, dtype=F32):
        ap = self.nc.dram_tensor(name, list(shape), dtype, kind="ExternalOutput").ap()
        self.outputs[name] = dtype
        return ap

    def dtmp(self, name, shape, dtype=F32, shared=False):
        kw = dict(addr_space="Shared") if shared else {}
        return self.nc.dram_tensor(name, list(shape), dtype, **kw).ap()

    def sb(self, name, shape, dtype):
        return self._ctx(self.nc.sbuf_tensor(name, list(shape), dtype))

    def ps(self, name, shape=(PB, 512), dtype=F32):
        return self._ctx(self.nc.psum_tensor(name, list(shape), dtype))

    def begin(self, preamble=True):
        self.block = self._ctx(self.nc.Block())
        if preamble and not SIM_MODE:
            self.qe.sem_clear(range(152, 256))
            for eng in [self.te, self.ve, self.se, self.ge]:
                eng.nop(cycle_cnt=60000)

    def end(self):
        for st in self._sems.values():
            if st.n > 0:
                self.qe.wait_ge(st.h, st.n)
        for cm in reversed(self._ctxs):
            cm.__exit__(None, None, None)
        self._ctxs = []
        self.nc.finalize()

    def dma(self, qeng, out, in_, sem):
        i = qeng.dma_start(out=out, in_=in_)
        i.then_inc(sem.h, 16)
        sem.n += 16
        return i

    def inc(self, instr, sem, by=1):
        instr.then_inc(sem.h, by)
        sem.n += by
        return instr

    def wait(self, eng, sem, lag=0):
        if sem.n - lag > 0:
            eng.wait_ge(sem.h, sem.n - lag)

    def barrier(self):
        for eng in (self.te, self.ve, self.se, self.ge, self.qe):
            for st in self._sems.values():
                if st.n > 0:
                    eng.wait_ge(st.h, st.n)


class FmBuf:

    def __init__(self, P, name, nkc, T, dtype=BF16, tensor=None, col0=0):
        self.nkc, self.T, self.dtype = nkc, T, dtype
        self.t = tensor if tensor is not None else P.sb(name, [PB, nkc * T], dtype)
        self.col0 = col0

    def ch(self, kc, t0=0, t1=None):
        t1 = self.T if t1 is None else t1
        c = self.col0 + kc * self.T
        if getattr(self, "_bc_f32", False):
            a, b = c + t0, c + t1
            assert a % 2 == 0 and b % 2 == 0
            return self.t[:, a // 2: b // 2].bitcast(BF16)
        return self.t[:, c + t0: c + t1]

    def view(self, kc_off, nkc, T=None):
        return FmBuf(None, None, nkc, self.T if T is None else T,
                     self.dtype, tensor=self.t, col0=self.col0 + kc_off * self.T)


class TileView:

    def __init__(self, base, off, T):
        self.base, self.off, self.T = base, off, T
        self.nkc, self.dtype = base.nkc, base.dtype

    def ch(self, kc, t0=0, t1=None):
        t1 = self.T if t1 is None else t1
        return self.base.ch(kc, self.off + t0, self.off + t1)


def load_fm(P, buf, hbm_ap, sem, nkc=None, qeng=None):
    nkc = buf.nkc if nkc is None else nkc
    q = qeng or P.qe
    for kc in range(nkc):
        P.dma(q, buf.ch(kc), hbm_ap[kc * PB:(kc + 1) * PB, :], sem)


def store_fm(P, buf, hbm_ap, sem, nkc=None, qeng=None, wait_sems=()):
    nkc = buf.nkc if nkc is None else nkc
    q = qeng or P.ge
    for ws in wait_sems:
        P.wait(q, ws)
    for kc in range(nkc):
        P.dma(q, hbm_ap[kc * PB:(kc + 1) * PB, :], buf.ch(kc), sem)



class PsumPool:

    def __init__(self, P, n=8):
        self.banks = [P.ps(f"psb{i}") for i in range(n)]
        self.pend = {}

    def wait_free(self, eng, i):
        if i in self.pend:
            sem, val = self.pend[i]
            if val > 0:
                eng.wait_ge(sem.h, val)

    def release(self, i, sem):
        self.pend[i] = (sem, sem.n)



class WStream:

    def __init__(self, P, name, w_ap, nmc, nkc, conv="v", single=False):
        self.P, self.w, self.nmc, self.nkc = P, w_ap, nmc, nkc
        self.single = single
        W = nkc * PB
        nb = 1 if single else 2
        self.f32 = P.sb(name + "_f", [PB, nb * W], F32)
        self.b16 = P.sb(name + "_b", [PB, nb * W], BF16)
        self.sL = [P.sem(name + "L0"), P.sem(name + "L1")]
        self.sC = P.sem(name + "C")
        self.conv = conv
        self.consumed = P.sem(name + "U")

    def load(self, mc):
        P = self.P
        par = 0 if self.single else mc % 2
        W = self.nkc * PB
        if self.single:
            P.wait(P.qe, self.sC)
        elif mc >= 2:
            P.wait(P.qe, self.sC, lag=1)
        src = self.w[mc].rearrange("p kc c -> p (kc c)")
        P.dma(P.qe, self.f32[:, par * W:(par + 1) * W], src, self.sL[par])

    def convert(self, mc, gate=None):
        P = self.P
        par = 0 if self.single else mc % 2
        W = self.nkc * PB
        eng = {"v": P.ve, "s": P.se, "g": P.ge}[self.conv]
        P.wait(eng, self.sL[par])
        if gate is not None:
            sem, val = gate
            if val > 0:
                eng.wait_ge(sem.h, val)
        i = eng.tensor_copy(self.b16[:, par * W:(par + 1) * W],
                            self.f32[:, par * W:(par + 1) * W])
        P.inc(i, self.sC)

    def tile(self, mc, kc):
        par = 0 if self.single else mc % 2
        W = self.nkc * PB
        return self.b16[:, par * W + kc * PB: par * W + (kc + 1) * PB]

    def done(self, mc, eng):
        self.P.inc(eng.engine_nop() if hasattr(eng, "engine_nop") else eng.nop(), self.consumed)



def emit_gemm(P, ws, xin, T, xout=None, act=None, bias=None, resid=None,
              sem_pref="gm", pool=None, banks=(0, 1, 2, 3), act_scale=1.0,
              post=None, nmc=None, scratch=None, out_dtype=None):
    te, ve, se = P.te, P.ve, P.se
    nkc, nmc = ws.nkc, (ws.nmc if nmc is None else nmc)
    sT = P.sem(sem_pref + "t")
    sE = P.sem(sem_pref + "e")
    ntt = (T + 511) // 512
    nbank = len(banks)
    slab_end = {}
    single = getattr(ws, "single", False)
    ws.load(0)
    ws.convert(0)
    for mc in range(nmc):
        if mc + 1 < nmc and not single:
            ws.load(mc + 1)
            ws.convert(mc + 1, gate=((sE, slab_end[mc - 1]) if mc >= 1 else None))
        P.wait(te, ws.sC, lag=(1 if (mc + 1 < nmc and not single) else 0))
        for tt in range(ntt):
            t0, t1 = tt * 512, min(T, tt * 512 + 512)
            n = t1 - t0
            bi = banks[(mc * ntt + tt) % nbank]
            bank = pool.banks[bi]
            pool.wait_free(te, bi)
            for kc in range(nkc):
                i = te.matmul(bank[:, 0:n], ws.tile(mc, kc), xin.ch(kc, t0, t1),
                              start=(kc == 0), stop=(kc == nkc - 1))
            P.inc(i, sT)
            if post is not None:
                post(mc, tt, bank, t0, t1, sT, sE)
            else:
                oc = xout.ch(mc, t0, t1)
                b = bias[:, mc:mc + 1] if bias is not None else 0.0
                fn = act if act is not None else (AF.Identity if bias is not None else AF.Copy)
                P.wait(se, sT)
                if resid is None:
                    P.inc(se.activation(oc, bank[:, 0:n], fn, bias=b,
                                        scale=act_scale), sE)
                else:
                    sc = scratch[:, 0:n]
                    P.inc(se.activation(sc, bank[:, 0:n], fn, bias=b,
                                        scale=act_scale), sE)
                    P.wait(ve, sE)
                    P.inc(ve.tensor_tensor(oc, sc, resid.ch(mc, t0, t1), ALU.add), sE)
            pool.release(bi, sE)
        slab_end[mc] = sE.n
        if mc + 1 < nmc and single:
            ws.load(mc + 1)
            ws.convert(mc + 1, gate=(sE, slab_end[mc]))



class LnScratch:
    def __init__(self, P, T, tag=""):
        self.onesf = P.sb("lns_onesf" + tag, [PB, 1], F32)
        self.onesb = P.sb("lns_onesb" + tag, [PB, 1], BF16)
        self.ones1 = P.sb("lns_ones1" + tag, [1, PB], F32)
        self.sq = P.sb("lns_sq" + tag, [PB, 1024], F32)
        self.rowm = P.sb("lns_rowm" + tag, [1, 512], F32)
        self.rowr = P.sb("lns_rowr" + tag, [1, 512], F32)
        self.row2 = P.sb("lns_row2" + tag, [1, 512], F32)
        self.mb = P.sb("lns_mb" + tag, [PB, 512], F32)
        self.rb = P.sb("lns_rb" + tag, [PB, 512], F32)
        self.xf = P.sb("lns_xf" + tag, [PB, 512], F32)
        self.init_done = False

    def init(self, P, sem):
        if self.init_done:
            return
        P.inc(P.ve.memset(self.onesf[:, :], 1.0), sem)
        P.inc(P.ve.memset(self.onesb[:, :], 1.0), sem)
        P.inc(P.ve.memset(self.ones1[:, :], 1.0), sem)
        self.init_done = True


def emit_ln(P, xin, xout, T, nkc=NKC, gamma=None, beta=None, eps=1e-5,
            pref="ln", lns=None, pool=None, lbanks=(4, 5, 3)):
    ia, ib, ic = lbanks
    pa, pb, pc = pool.banks[ia], pool.banks[ib], pool.banks[ic]
    """LayerNorm over the feature (partition) axis. xin: FmBuf fp32 or bf16;
    xout: FmBuf bf16 (or fp32). gamma/beta: SBUF f32 [128, nkc] per-feature.
    pa/pb: psum banks [128, 512]."""
    te, ve, se = P.te, P.ve, P.se
    sV = P.sem(pref + "v")
    sT = P.sem(pref + "t")
    sS = P.sem(pref + "s")
    lns.init(P, sV)
    ones_in = lns.onesf if xin.dtype == F32 else lns.onesb
    nfeat = float(nkc * PB)
    ntt = (T + 511) // 512
    for tt in range(ntt):
        t0, t1 = tt * 512, min(T, tt * 512 + 512)
        n = t1 - t0
        P.wait(te, sV)
        pool.wait_free(te, ia)
        pool.wait_free(te, ib)
        for kc in range(nkc):
            te.matmul(pa[0:1, 0:n], ones_in[:, :], xin.ch(kc, t0, t1),
                      start=(kc == 0), stop=(kc == nkc - 1))
        sq_base = sT.n
        for kc in range(nkc):
            half = (kc % 2) * 512
            if kc >= 2:
                se.wait_ge(sT.h, sq_base + kc - 1)
            P.inc(se.activation(lns.sq[:, half:half + n], xin.ch(kc, t0, t1),
                                AF.Square), sS)
            P.wait(te, sS)
            i2 = te.matmul(pb[0:1, 0:n], lns.onesf[:, :], lns.sq[:, half:half + n],
                           start=(kc == 0), stop=(kc == nkc - 1))
            P.inc(i2, sT)
        P.wait(ve, sT)
        P.inc(ve.tensor_scalar_mul(lns.rowm[:, 0:n], pa[0:1, 0:n], 1.0 / nfeat), sV)
        P.inc(ve.tensor_scalar_mul(lns.rowr[:, 0:n], pb[0:1, 0:n], 1.0 / nfeat), sV)
        P.wait(ve, sV)
        P.inc(ve.tensor_tensor(lns.row2[:, 0:n], lns.rowm[:, 0:n],
                               lns.rowm[:, 0:n], ALU.mult), sV)
        P.wait(ve, sV)
        P.inc(ve.tensor_tensor(lns.rowr[:, 0:n], lns.rowr[:, 0:n],
                               lns.row2[:, 0:n], ALU.subtract), sV)
        P.wait(ve, sV)
        P.inc(ve.tensor_scalar_add(lns.rowr[:, 0:n], lns.rowr[:, 0:n], eps), sV)
        P.wait(se, sV)
        P.inc(se.activation(lns.rowr[:, 0:n], lns.rowr[:, 0:n], AF.Sqrt), sS)
        P.wait(ve, sS)
        P.inc(ve.reciprocal(lns.rowr[:, 0:n], lns.rowr[:, 0:n]), sV)
        P.wait(te, sV)
        pool.wait_free(te, ic)
        P.inc(te.matmul(pc[:, 0:n], lns.ones1[0:1, :], lns.rowm[:, 0:n],
                        start=True, stop=True), sT)
        P.wait(ve, sT)
        P.inc(ve.tensor_copy(lns.mb[:, 0:n], pc[:, 0:n]), sV)
        P.wait(te, sV)
        P.inc(te.matmul(pc[:, 0:n], lns.ones1[0:1, :], lns.rowr[:, 0:n],
                        start=True, stop=True), sT)
        P.wait(ve, sT)
        P.inc(ve.tensor_copy(lns.rb[:, 0:n], pc[:, 0:n]), sV)
        P.wait(ve, sV)
        for kc in range(nkc):
            P.inc(ve.tensor_tensor(lns.xf[:, 0:n], xin.ch(kc, t0, t1),
                                   lns.mb[:, 0:n], ALU.subtract), sV)
            P.wait(ve, sV)
            if gamma is None:
                P.inc(ve.tensor_tensor(xout.ch(kc, t0, t1), lns.xf[:, 0:n],
                                       lns.rb[:, 0:n], ALU.mult), sV)
            else:
                P.inc(ve.tensor_tensor(lns.xf[:, 0:n], lns.xf[:, 0:n],
                                       lns.rb[:, 0:n], ALU.mult), sV)
                P.wait(ve, sV)
                P.inc(ve.tensor_scalar(xout.ch(kc, t0, t1), lns.xf[:, 0:n],
                                       gamma[:, kc:kc + 1], beta[:, kc:kc + 1],
                                       ALU.mult, ALU.add), sV)
            P.wait(ve, sV)





class Common:
    def __init__(self, P, T, npsum=8, lnT=None):
        self.pool = PsumPool(P, npsum)
        self.lns = LnScratch(P, lnT or T)
        self.gsc = P.sb("gemm_scr", [PB, 512], F32)


def run_prog(P, in_maps):
    res = run_bass_kernel_spmd(P.nc, in_maps, core_ids=list(range(NCORES)))
    return res.results


def bfa(x):
    return np.ascontiguousarray(x, dtype=BF)



def build_l0a():
    P = Prog("l0a")
    x_h = P.din("x", [D, TC], F32)
    g1 = P.din("ln1g", [PB, NKC], F32)
    b1 = P.din("ln1b", [PB, NKC], F32)
    wu = {}
    wv = {}
    bqkv = {}
    for nm in ("q", "k", "v"):
        wu[nm] = P.din(f"{nm}u", [6, PB, NKC, PB], F32)
        wv[nm] = P.din(f"{nm}v", [NKC, PB, 6, PB], F32)
        bqkv[nm] = P.din(f"{nm}b", [PB, NKC], F32)
    xn_h = P.dout("xn", [D, TC], BF16)
    o_h = {nm: P.dout(nm + "t", [D, TC], BF16) for nm in ("q", "k", "v")}
    P.begin()
    C = Common(P, TC)
    sIn = P.sem("in")
    x = FmBuf(P, "xs", NKC, TC, F32)
    xn = FmBuf(P, "xnb", NKC, TC, BF16)
    x1 = FmBuf(None, None, NKC, TC, BF16, tensor=x.t, col0=0)
    x1._bc_f32 = True
    ot = FmBuf(None, None, NKC, TC, BF16, tensor=x.t, col0=NKC * TC)
    ot._bc_f32 = True
    pt = FmBuf(P, "ptb", 6, TC, BF16)
    gam = P.sb("g1s", [PB, NKC], F32)
    bet = P.sb("b1s", [PB, NKC], F32)
    load_fm(P, x, x_h, sIn)
    P.dma(P.qe, gam[:, :], g1, sIn)
    P.dma(P.qe, bet[:, :], b1, sIn)
    for e in (P.te, P.ve, P.se):
        P.wait(e, sIn)
    emit_ln(P, x, xn, TC, pref="lnA", lns=C.lns, pool=C.pool)
    sSt = P.sem("st")
    store_fm(P, xn, xn_h, sSt, wait_sems=[P.sem("lnAv")])
    emit_ln(P, xn, x1, TC, gamma=gam, beta=bet, pref="lnB", lns=C.lns,
            pool=C.pool)
    ws1 = WStream(P, "w1", wu["q"], 6, NKC)
    ws2 = WStream(P, "w2", wv["q"], NKC, 6)
    bias = P.sb("biasq", [PB, NKC], F32)
    prev = None
    for i, nm in enumerate(("q", "k", "v")):
        ws1.w = wu[nm]
        ws2.w = wv[nm]
        if prev is not None:
            P.wait(P.qe, P.sem(f"g{prev}2e"))
            P.wait(P.se, P.sem("st"))
            P.wait(P.ve, P.sem("st"))
        P.dma(P.qe, bias[:, :], bqkv[nm], P.sem(f"inb_{nm}"))
        P.wait(P.se, P.sem(f"inb_{nm}"))
        prev = nm
        emit_gemm(P, ws1, x1, TC, xout=pt, sem_pref=f"g{nm}1", pool=C.pool)
        emit_gemm(P, ws2, pt, TC, xout=ot, bias=bias, sem_pref=f"g{nm}2",
                  pool=C.pool)
        store_fm(P, ot, o_h[nm], sSt, wait_sems=[P.sem(f"g{nm}2e")])
    P.end()
    return P


def stage_l0a(inputs, core):
    b, half = core // 2, core % 2
    sl = slice(half * TC, (half + 1) * TC)
    x = np.asarray(inputs["L0_attn_hidden_states"][b, sl], np.float32)
    d = {
        "x": fmh(x),
        "ln1g": np.asarray(inputs["L0_attn_ln1_g"], np.float32).reshape(NKC, PB).T.copy(),
        "ln1b": np.asarray(inputs["L0_attn_ln1_b"], np.float32).reshape(NKC, PB).T.copy(),
    }
    for nm in ("q", "k", "v"):
        U = np.asarray(inputs[f"L0_attn_{nm}_U"], np.float32).reshape(D, 768)
        V = np.asarray(inputs[f"L0_attn_{nm}_V"], np.float32)
        bb = np.asarray(inputs[f"L0_attn_{nm}_b"], np.float32).reshape(-1)
        Vbd = np.zeros((768, D), np.float32)
        for h in range(16):
            Vbd[h * 48:(h + 1) * 48, h * 64:(h + 1) * 64] = V[h]
        d[nm + "u"] = pack_w(U)
        d[nm + "v"] = pack_w(Vbd)
        d[nm + "b"] = bb.reshape(NKC, PB).T.copy()
    return d





def emit_attn(P, heads, SQ, pool, pref="at", obank_i=6, sbank_i=(4, 5),
              exp_scale=0.125):
    te, ve, se = P.te, P.ve, P.se
    sT = P.sem(pref + "t")
    sE = P.sem(pref + "e")
    sV = P.sem(pref + "v")
    sc = heads[0]["sc"]
    ob = pool.banks[obank_i]
    for hd in heads:
        Qv, Kv, VA, sched, cb, ctx = hd["Q"], hd["K"], hd["VA"], hd["sched"], hd["cb"], hd["ctx"]
        nr = hd.get("nr", 64)
        for qt, kbs in sched:
            q0, q1 = qt * 512, min(SQ, qt * 512 + 512)
            n = q1 - q0
            pool.wait_free(te, obank_i)
            first = True
            for j, (kb, mask, var, minvar) in enumerate(kbs):
                sb_i = sbank_i[j % 2]
                bank = pool.banks[sb_i]
                pool.wait_free(te, sb_i)
                i = te.matmul(bank[:, 0:n], Kv[var][:, kb * PB:(kb + 1) * PB],
                              Qv[var][:, q0:q1], start=True, stop=True)
                P.inc(i, sT)
                if minvar is not None:
                    sb2 = sbank_i[(j + 1) % 2]
                    pool.wait_free(te, sb2)
                    b2 = pool.banks[sb2]
                    i2 = te.matmul(b2[:, 0:n], Kv[minvar][:, kb * PB:(kb + 1) * PB],
                                   Qv[minvar][:, q0:q1], start=True, stop=True)
                    P.inc(i2, sT)
                    P.wait(ve, sT)
                    P.inc(ve.tensor_copy(sc["minf"][:, 0:n], b2[:, 0:n]), sV)
                    P.wait(ve, sV)
                    P.inc(ve.tensor_tensor(bank[:, 0:n], bank[:, 0:n],
                                           sc["minf"][:, 0:n], ALU.min), sV)
                    pool.release(sb2, sV)
                    P.wait(se, sV)
                if mask is not None:
                    P.wait(ve, sT)
                    P.inc(ve.tensor_tensor(bank[:, 0:n], bank[:, 0:n], mask,
                                           ALU.add), sV)
                    P.wait(se, sV)
                else:
                    P.wait(se, sT)
                pt = sc["pt"][:, (j % 2) * 512:(j % 2) * 512 + n]
                if j >= 2:
                    se.wait_ge(sT.h, sT.n)
                P.inc(se.activation(pt, bank[:, 0:n], AF.Exp, bias=cb,
                                    scale=exp_scale), sE)
                pool.release(sb_i, sE)
                P.wait(te, sE)
                iav = te.matmul(ob[0:65, 0:n], VA[:, kb * 65:(kb + 1) * 65], pt,
                                start=first, stop=(j == len(kbs) - 1))
                first = False
            P.inc(iav, sT)
            P.wait(ve, sT)
            P.inc(ve.reciprocal(sc["rrow"][:, 0:n], ob[64:65, 0:n]), sV)
            P.wait(ve, sV)
            P.inc(ve.tensor_copy(sc["r16"][:, 0:n], sc["rrow"][:, 0:n]), sV)
            P.wait(se, sT)
            P.inc(se.activation(sc["osb"][0:64, 0:n], ob[0:64, 0:n], AF.Copy), sE)
            bi = sbank_i[0]
            pool.wait_free(te, bi)
            P.wait(te, sV)
            ib = te.matmul(pool.banks[bi][0:64, 0:n], sc["ones1b"][0:1, 0:64],
                           sc["r16"][:, 0:n], start=True, stop=True)
            P.inc(ib, sT)
            P.wait(ve, sT)
            P.wait(ve, sE)
            P.inc(ve.tensor_tensor(ctx(q0, q1), sc["osb"][0:64, 0:n],
                                   pool.banks[bi][0:64, 0:n], ALU.mult), sV)
            pool.release(bi, sV)
            pool.release(obank_i, sE)


def attn_scratch(P, tag="", minf=False):
    sc = {
        "pt": P.sb("at_pt" + tag, [PB, 1024], BF16),
        "rrow": P.sb("at_rrow" + tag, [1, 512], F32),
        "r16": P.sb("at_r16" + tag, [1, 512], BF16),
        "ones1b": P.sb("at_ones1b" + tag, [1, PB], BF16),
        "osb": P.sb("at_osb" + tag, [PB, 512], BF16),
    }
    if minf:
        sc["minf"] = P.sb("at_minf" + tag, [PB, 512], F32)
    return sc


def emit_head_bound(P, QT, KT, SQ, SK, cb_out, pool, sc2, pref="hb",
                    extra_scale=1.0, bank_i=7):
    te, ve, se = P.te, P.ve, P.se
    sT = P.sem(pref + "t")
    sV = P.sem(pref + "v")
    sS = P.sem(pref + "s")
    bank = pool.banks[bank_i]
    col = 0
    for name, T_, X in (("q", SQ, QT), ("k", SK, KT)):
        ntt = (T_ + 511) // 512
        for tt in range(ntt):
            t0, t1 = tt * 512, min(T_, tt * 512 + 512)
            n = t1 - t0
            P.wait(se, sT)
            P.inc(se.activation(sc2["sq"][:, 0:n], X[:, t0:t1], AF.Square), sS)
            P.wait(te, sS)
            pool.wait_free(te, bank_i)
            i = te.matmul(bank[0:1, 0:n], sc2["onesb64"][:, :], sc2["sq"][:, 0:n],
                          start=True, stop=True)
            P.inc(i, sT)
            P.wait(ve, sT)
            P.inc(ve.reduce_max(sc2["red"][:, col:col + 1], bank[0:1, 0:n],
                                axis=AX.X), sV)
            pool.release(bank_i, sV)
            col += 1
    nq = (SQ + 511) // 512
    P.wait(ve, sV)
    P.inc(ve.reduce_max(sc2["sc1"][:, 0:1], sc2["red"][:, 0:nq], axis=AX.X), sV)
    P.wait(ve, sV)
    P.inc(ve.reduce_max(sc2["sc1"][:, 1:2], sc2["red"][:, nq:col], axis=AX.X), sV)
    P.wait(ve, sV)
    P.inc(ve.tensor_tensor(sc2["sc1"][:, 2:3], sc2["sc1"][:, 0:1],
                           sc2["sc1"][:, 1:2], ALU.mult), sV)
    P.wait(se, sV)
    P.inc(se.activation(sc2["sc1"][:, 2:3], sc2["sc1"][:, 2:3], AF.Sqrt), sS)
    P.wait(ve, sS)
    P.inc(ve.tensor_scalar_mul(sc2["sc1"][:, 2:3], sc2["sc1"][:, 2:3],
                               -0.125 * extra_scale), sV)
    P.wait(ve, sV)
    P.inc(ve.tensor_copy(sc2["sc1b"][:, 0:1], sc2["sc1"][:, 2:3]), sV)
    pool.wait_free(te, bank_i)
    P.wait(te, sV)
    i = te.matmul(bank[:, 0:1], sc2["ones1b"][0:1, :], sc2["sc1b"][:, 0:1],
                  start=True, stop=True)
    P.inc(i, sT)
    P.wait(ve, sT)
    P.inc(ve.tensor_copy(cb_out[:, :], bank[:, 0:1]), sV)
    pool.release(bank_i, sV)


def hb_scratch(P, tag=""):
    sc2 = {
        "sq": P.sb("hb_sq" + tag, [64, 512], BF16),
        "red": P.sb("hb_red" + tag, [1, 32], F32),
        "sc1": P.sb("hb_sc1" + tag, [1, 4], F32),
        "sc1b": P.sb("hb_sc1b" + tag, [1, 4], BF16),
        "onesb64": P.sb("hb_onesb64" + tag, [64, 1], BF16),
        "ones1b": P.sb("hb_ones1bx" + tag, [1, PB], BF16),
    }
    return sc2



def build_l0b():
    P = Prog("l0b")
    q_h = P.din("q", [8, 64, S], BF16)
    k_h = P.din("k", [8, 64, S], BF16)
    va_h = P.din("va", [8, PB, 32, 65], BF16)
    mk_h = P.din("mk", [PB, 4, 512], F32)
    ctx_h = P.dout("ctx", [8, 64, S], BF16)
    P.begin()
    pool = PsumPool(P)
    sc = attn_scratch(P)
    sc2 = hb_scratch(P)
    sIn = P.sem("in")
    sSt = P.sem("st")
    QT = P.sb("QT", [64, S], BF16)
    KT = P.sb("KT", [64, S], BF16)
    VA = P.sb("VAs", [PB, 32 * 65], BF16)
    CX = P.sb("CX", [64, S], BF16)
    MK = P.sb("MK", [PB, 4 * 512], F32)
    cb = P.sb("cb", [PB, 1], F32)
    P.dma(P.qe, MK[:, :], mk_h.rearrange("p m c -> p (m c)"), sIn)
    P.inc(P.ve.memset(sc["ones1b"][:, :], 1.0), P.sem("ini"))
    P.inc(P.ve.memset(sc2["onesb64"][:, :], 1.0), P.sem("ini"))
    P.inc(P.ve.memset(sc2["ones1b"][:, :], 1.0), P.sem("ini"))
    sched = []
    for qt in range(8):
        kbs = []
        for kb in range(4 * qt + 4):
            off = kb - 4 * qt
            if off >= 0:
                kbs.append((kb, ("m", off), 0, None))
            else:
                kbs.append((kb, None, 0, None))
        sched.append((qt, kbs))
    sPrev = None
    for h in range(8):
        sH = P.sem("hh")
        P.dma(P.qe, QT[:, :], q_h[h], sH)
        P.dma(P.qe, KT[:, :], k_h[h], sH)
        P.dma(P.qe, VA[:, :], va_h[h].rearrange("p kb c -> p (kb c)"), sH)
        for e in (P.te, P.ve, P.se):
            P.wait(e, sH)
            P.wait(e, sIn)
            P.wait(e, P.sem("ini"))
        emit_head_bound(P, QT[:, :], KT[:, :], S, S, cb[:, :], pool, sc2,
                        pref="hb")
        hd = {
            "Q": [QT[:, :]], "K": [KT[:, :]], "VA": VA[:, :],
            "sched": [(qt, [(kb, (MK[:, 512 * m[1]:512 * m[1] + 512] if isinstance(m, tuple) else None), v, mv)
                            for (kb, m, v, mv) in kbs])
                      for (qt, kbs) in sched],
            "cb": cb[:, :], "ctx": lambda a, b: CX[:, a:b], "sc": sc,
        }
        emit_attn(P, [hd], S, pool, pref="at")
        P.wait(P.ge, P.sem("atv"))
        P.dma(P.ge, ctx_h[h], CX[:, :], sSt)
        P.wait(P.qe, sSt)
    P.end()
    return P


def stage_l0b(inputs, core, l0a_outs):
    b, half = core // 2, core % 2
    e = l0a_outs
    hsl = slice(half * 8 * 64, (half + 1) * 8 * 64)
    qf = np.concatenate([e[2 * b][k] for k in ("q",)], axis=1) if False else None
    q = np.concatenate([e[2 * b]["qt"], e[2 * b + 1]["qt"]], axis=1)
    k = np.concatenate([e[2 * b]["kt"], e[2 * b + 1]["kt"]], axis=1)
    v = np.concatenate([e[2 * b]["vt"], e[2 * b + 1]["vt"]], axis=1)
    q8 = q[hsl].reshape(8, 64, S)
    k8 = k[hsl].reshape(8, 64, S)
    v8 = v[hsl].reshape(8, 64, S)
    va = np.concatenate([v8.transpose(0, 2, 1),
                         np.ones((8, S, 1), BF)], axis=2)
    masks = np.zeros((4, PB, 512), np.float32)
    for off in range(4):
        k_idx = np.arange(PB)[:, None] + off * PB
        q_idx = np.arange(512)[None, :]
        masks[off][k_idx > q_idx] = NEG
    va4 = np.ascontiguousarray(va.reshape(8, 32, PB, 65).transpose(0, 2, 1, 3))
    return {"q": np.ascontiguousarray(q8), "k": np.ascontiguousarray(k8),
            "va": va4, "mk": masks}



def emit_tt3(P, out, a, b, op, pref="tt"):
    sV = P.sem(pref + "v")
    for kc in range(out.nkc):
        P.inc(P.ve.tensor_tensor(out.ch(kc), a.ch(kc), b.ch(kc), op), sV)
    return sV


def emit_act(P, out, a, fn, pref="ac"):
    sS = P.sem(pref + "s")
    for kc in range(out.nkc):
        P.inc(P.se.activation(out.ch(kc), a.ch(kc), fn), sS)
    return sS


def emit_elu1(P, buf, T, scratch, pref="el"):
    sV = P.sem(pref + "v")
    sS = P.sem(pref + "s")
    for kc in range(buf.nkc):
        for t0 in range(0, T, 512):
            t1 = min(T, t0 + 512)
            n = t1 - t0
            c = buf.ch(kc, t0, t1)
            P.inc(P.ve.tensor_scalar_min(scratch["e1"][:, 0:n], c, 0.0), sV)
            P.inc(P.ve.tensor_scalar_max(scratch["e2"][:, 0:n], c, 0.0), sV)
            P.wait(P.se, sV)
            P.inc(P.se.activation(scratch["e1"][:, 0:n], scratch["e1"][:, 0:n],
                                  AF.Exp), sS)
            P.wait(P.ve, sS)
            P.inc(P.ve.tensor_tensor(c, scratch["e1"][:, 0:n],
                                     scratch["e2"][:, 0:n], ALU.add), sV)
            P.wait(P.ve, sV)


def emit_combine(P, xf32, s_b, g_b, pref="cb", scratch=None):
    sV = P.sem(pref + "v")
    for kc in range(xf32.nkc):
        for t0 in range(0, xf32.T, 512):
            t1 = min(xf32.T, t0 + 512)
            n = t1 - t0
            P.inc(P.ve.tensor_tensor(scratch[:, 0:n], s_b.ch(kc, t0, t1),
                                     g_b.ch(kc, t0, t1), ALU.mult), sV)
            P.wait(P.ve, sV)
            P.inc(P.ve.tensor_tensor(xf32.ch(kc, t0, t1), xf32.ch(kc, t0, t1),
                                     scratch[:, 0:n], ALU.add), sV)
            P.wait(P.ve, sV)
    return sV


def emit_vaug(P, vfm, VA, nkb, identity, pool, pref="tv", bank_i=7, t_off=0,
              r0=0):
    te, ve = P.te, P.ve
    sT = P.sem(pref + "t")
    sV = P.sem(pref + "v")
    P.inc(ve.memset(VA[:, :], 1.0), sV)
    P.wait(te, sV)
    bank = pool.banks[bank_i]
    idap = identity[r0:r0 + 64, r0:r0 + 64]
    for kb in range(nkb):
        pool.wait_free(te, bank_i)
        i = te.transpose(bank.bitcast(BF16)[0:PB, 0:64],
                         vfm[:, t_off + kb * PB: t_off + (kb + 1) * PB], idap)
        P.inc(i, sT)
        P.wait(ve, sT)
        P.inc(ve.tensor_copy(VA[:, kb * 65: kb * 65 + 64], bank.bitcast(BF16)[0:PB, 0:64]), sV)
        pool.release(bank_i, sV)



def build_l0c():
    P = Prog("l0c")
    ctx_h = P.din("ctx", [D, TC], BF16)
    xn_h = P.din("xn", [D, TC], BF16)
    wou_h = P.din("wou", [6, PB, NKC, PB], F32)
    wov_h = P.din("wov", [NKC, PB, 6, PB], F32)
    ob_h = P.din("ob", [PB, NKC], F32)
    g2_h = P.din("ln2g", [PB, NKC], F32)
    b2_h = P.din("ln2b", [PB, NKC], F32)
    f1u_h = P.din("f1u", [4, PB, NKC, PB], F32)
    f1v_h = P.din("f1v", [32, PB, 4, PB], F32)
    f1b_h = P.din("f1b", [PB, 32], F32)
    f2u_h = P.din("f2u", [4, PB, 32, PB], F32)
    f2v_h = P.din("f2v", [NKC, PB, 4, PB], F32)
    f2b_h = P.din("f2b", [PB, NKC], F32)
    wsh1_h = P.din("wsh1", [32, PB, NKC, PB], F32)
    bsh1_h = P.din("bsh1", [PB, 32], F32)
    wsh2_h = P.din("wsh2", [NKC, PB, 32, PB], F32)
    bsh2_h = P.din("bsh2", [PB, NKC], F32)
    wr_h = P.din("wr", [1, PB, NKC, PB], F32)
    s_h = P.dout("s", [D, TC], BF16)
    sh_h = P.dout("sh", [D, TC], BF16)
    lg_h = P.dout("lg", [PB, TC], F32)
    P.begin()
    C = Common(P, TC)
    pool = C.pool
    sIn = P.sem("in")
    sSt = P.sem("st")
    ctx = FmBuf(P, "ctxb", NKC, TC, BF16)
    xn = FmBuf(P, "xnb", NKC, TC, BF16)
    t768 = FmBuf(P, "t768", 6, TC, BF16)
    hbuf = FmBuf(P, "hbuf", NKC, TC, BF16)
    x2 = FmBuf(P, "x2b", NKC, TC, BF16)
    hm = FmBuf(None, None, 32, 512, BF16, tensor=ctx.t)
    t512a = FmBuf(None, None, 4, 512, BF16, tensor=t768.t, col0=0)
    t512b = FmBuf(None, None, 4, 512, BF16, tensor=t768.t, col0=2048)
    outv = FmBuf(None, None, NKC, 512, BF16, tensor=t768.t, col0=4096)
    bias = P.sb("biasx", [PB, 32], F32)
    gam = P.sb("g2s", [PB, NKC], F32)
    bet = P.sb("b2s", [PB, NKC], F32)
    load_fm(P, ctx, ctx_h, sIn)
    load_fm(P, xn, xn_h, sIn)
    P.dma(P.qe, gam[:, :], g2_h, sIn)
    P.dma(P.qe, bet[:, :], b2_h, sIn)
    for e in (P.te, P.ve, P.se):
        P.wait(e, sIn)
    ws = WStream(P, "w", wou_h, 6, NKC)
    wsb = WStream(P, "wb", f2u_h, 4, 32, single=True)
    emit_gemm(P, ws, ctx, TC, xout=t768, sem_pref="go1", pool=pool)
    P.barrier()
    ws.w, ws.nmc, ws.nkc = wov_h, NKC, 6
    sB = P.sem("inb1")
    P.dma(P.qe, bias[:, 0:NKC], ob_h, sB)
    P.wait(P.se, sB)
    emit_gemm(P, ws, t768, TC, xout=hbuf, bias=bias, resid=xn,
              sem_pref="go2", pool=pool, scratch=C.gsc)
    P.barrier()
    emit_ln(P, hbuf, x2, TC, gamma=gam, beta=bet, pref="ln2", lns=C.lns, pool=pool)
    P.barrier()
    sB2 = P.sem("inb2")
    P.dma(P.qe, bias[:, :], f1b_h, sB2)
    P.wait(P.se, sB2)
    for tt in range(TC // 512):
        t0 = tt * 512
        ws.w, ws.nmc, ws.nkc = f1u_h, 4, NKC
        emit_gemm(P, ws, TileView(x2, t0, 512), 512, xout=t512a,
                  sem_pref="gm1", pool=pool)
        P.barrier()
        ws.w, ws.nmc, ws.nkc = f1v_h, 32, 4
        emit_gemm(P, ws, t512a, 512, xout=hm, bias=bias,
                  act=AF.Gelu_apprx_tanh, sem_pref="gm2", pool=pool)
        P.barrier()
        wsb.w, wsb.nmc, wsb.nkc = f2u_h, 4, 32
        emit_gemm(P, wsb, hm, 512, xout=t512b, sem_pref="gm3", pool=pool)
        P.barrier()
        sB3 = P.sem("inbt")
        P.dma(P.qe, bias[:, 0:NKC], f2b_h, sB3)
        P.wait(P.se, sB3)
        ws.w, ws.nmc, ws.nkc = f2v_h, NKC, 4
        emit_gemm(P, ws, t512b, 512, xout=outv, bias=bias,
                  resid=TileView(hbuf, t0, 512), sem_pref="gm4",
                  pool=pool, scratch=C.gsc)
        store_fm(P, outv, s_h[:, t0:t0 + 512], sSt,
                 wait_sems=[P.sem("gm4e")])
        P.barrier()
        sB4 = P.sem("inbt")
        P.dma(P.qe, bias[:, :], bsh1_h, sB4)
        P.wait(P.se, sB4)
        ws.w, ws.nmc, ws.nkc = wsh1_h, 32, NKC
        emit_gemm(P, ws, TileView(xn, t0, 512), 512, xout=hm, bias=bias,
                  act=AF.Gelu_apprx_tanh, sem_pref="gs1", pool=pool)
        P.barrier()
        sB5 = P.sem("inbt")
        P.dma(P.qe, bias[:, 0:NKC], bsh2_h, sB5)
        P.wait(P.se, sB5)
        wsb.w, wsb.nmc, wsb.nkc = wsh2_h, NKC, 32
        emit_gemm(P, wsb, hm, 512, xout=outv, bias=bias,
                  sem_pref="gs2", pool=pool)
        store_fm(P, outv, sh_h[:, t0:t0 + 512], sSt,
                 wait_sems=[P.sem("gs2e")])
        P.barrier()
        sB6 = P.sem("inbt")
        P.dma(P.qe, bias[:, :], f1b_h, sB6)
        P.wait(P.se, sB6)
    ws.w, ws.nmc, ws.nkc = wr_h, 1, NKC
    sStr = P.sem("strr")

    def postr(mc, tt, bank, a0, a1, sTl, sEl):
        n = a1 - a0
        P.wait(P.se, sTl)
        if tt >= 1:
            P.wait(P.se, sStr)
        P.inc(P.se.activation(C.gsc[:, 0:n], bank[:, 0:n], AF.Copy), sEl)
        P.wait(P.ge, sEl)
        P.dma(P.ge, lg_h[:, a0:a1], C.gsc[:, 0:n], sStr)
    emit_gemm(P, ws, xn, TC, sem_pref="gr", pool=pool, post=postr)
    P.end()
    return P


def stage_l0c(inputs, core, ctx_fm):
    d = {
        "ctx": ctx_fm,
        "wou": pack_w(np.asarray(inputs["L0_attn_out_U"], np.float32)),
        "wov": pack_w(np.asarray(inputs["L0_attn_out_V"], np.float32)),
        "ob": np.asarray(inputs["L0_attn_out_b"], np.float32).reshape(NKC, PB).T.copy(),
        "ln2g": np.asarray(inputs["L0_attn_ln2_g"], np.float32).reshape(NKC, PB).T.copy(),
        "ln2b": np.asarray(inputs["L0_attn_ln2_b"], np.float32).reshape(NKC, PB).T.copy(),
        "f1u": pack_w(np.asarray(inputs["L0_attn_fc1_U"], np.float32)),
        "f1v": pack_w(np.asarray(inputs["L0_attn_fc1_V"], np.float32)),
        "f1b": np.asarray(inputs["L0_attn_fc1_b"], np.float32).reshape(32, PB).T.copy(),
        "f2u": pack_w(np.asarray(inputs["L0_attn_fc2_U"], np.float32)),
        "f2v": pack_w(np.asarray(inputs["L0_attn_fc2_V"], np.float32)),
        "f2b": np.asarray(inputs["L0_attn_fc2_b"], np.float32).reshape(NKC, PB).T.copy(),
        "wsh1": pack_w(np.asarray(inputs["L0_ffn_Wfc_s"], np.float32)),
        "bsh1": np.asarray(inputs["L0_ffn_bfc_s"], np.float32).reshape(32, PB).T.copy(),
        "wsh2": pack_w(np.asarray(inputs["L0_ffn_Wproj_s"], np.float32)),
        "bsh2": np.asarray(inputs["L0_ffn_bproj_s"], np.float32).reshape(NKC, PB).T.copy(),
        "wr": pack_w(np.pad(np.asarray(inputs["L0_ffn_Wr"], np.float32),
                            ((0, 0), (0, 120)))),
    }
    return d



TE_ALL = B * S


class WResident:

    single = False

    def __init__(self, P, name, w_ap, nmc, nkc, conv="v", stage=None,
                 stage_single=False):
        self.P, self.w, self.nmc, self.nkc = P, w_ap, nmc, nkc
        self.stage_single = stage_single
        W = nkc * PB
        self.stage = stage if stage is not None else P.sb(name + "_f", [PB, 2 * W], F32)
        self.arena = P.sb(name + "_a", [PB, nmc * W], BF16)
        self.sL = [P.sem(name + "L0"), P.sem(name + "L1")]
        self.sC = P.sem(name + "C")
        self.conv = conv
        self.loaded = False

    def preload(self):
        P = self.P
        W = self.nkc * PB
        eng = {"v": P.ve, "s": P.se, "g": P.ge}[self.conv]
        for mc in range(self.nmc):
            par = 0 if self.stage_single else mc % 2
            if self.stage_single:
                P.wait(P.qe, self.sC)
            elif mc >= 2:
                P.wait(P.qe, self.sC, lag=1)
            srcap = self.w[mc].rearrange("p kc c -> p (kc c)")
            P.dma(P.qe, self.stage[:, par * W:(par + 1) * W], srcap, self.sL[par])
            P.wait(eng, self.sL[par])
            i = eng.tensor_copy(self.arena[:, mc * W:(mc + 1) * W],
                                self.stage[:, par * W:(par + 1) * W])
            P.inc(i, self.sC)
        self.loaded = True

    def load(self, mc):
        if not self.loaded:
            self.preload()

    def convert(self, mc, gate=None):
        pass

    def tile(self, mc, kc):
        W = self.nkc * PB
        return self.arena[:, mc * W + kc * PB: mc * W + (kc + 1) * PB]


def emit_gate(P, lgt_tile, gate_row, e_idx, pool, gsc2, identity, pref="gt",
              bank_i=7):
    ve, se, te = P.ve, P.se, P.te
    sV = P.sem(pref + "v")
    sS = P.sem(pref + "s")
    sT = P.sem(pref + "t")
    g16 = gsc2["g16"]
    for st in range(4):
        L = lgt_tile[:, st * 8:(st + 1) * 8]
        m1 = gsc2["m1"]
        P.inc(ve.reduce_max(m1[:, 0:1], L, axis=AX.X), sV)
        P.wait(ve, sV)
        P.inc(ve.tensor_scalar_mul(m1[:, 1:2], m1[:, 0:1], -1.0), sV)
        P.wait(se, sV)
        P.inc(se.activation(gsc2["p"][:, 0:8], L, AF.Exp, bias=m1[:, 1:2]), sS)
        P.wait(ve, sS)
        P.inc(ve.reduce_sum(m1[:, 2:3], gsc2["p"][:, 0:8], axis=AX.X), sV)
        P.wait(ve, sV)
        P.inc(ve.reciprocal(m1[:, 3:4], m1[:, 2:3]), sV)
        P.wait(ve, sV)
        P.inc(ve.tensor_scalar_mul(gsc2["p"][:, 0:8], gsc2["p"][:, 0:8],
                                   m1[:, 3:4]), sV)
        P.wait(ve, sV)
        P.inc(ve.reduce_max(m1[:, 4:5], gsc2["p"][:, 0:8], axis=AX.X), sV)
        P.wait(ve, sV)
        P.inc(ve.tensor_single_scalar(gsc2["q"][:, 0:8], gsc2["p"][:, 0:8],
                                      m1[:, 4:5], ALU.is_lt), sV)
        P.wait(ve, sV)
        P.inc(ve.tensor_tensor(gsc2["q"][:, 0:8], gsc2["q"][:, 0:8],
                               gsc2["p"][:, 0:8], ALU.mult), sV)
        P.wait(ve, sV)
        P.inc(ve.reduce_max(m1[:, 5:6], gsc2["q"][:, 0:8], axis=AX.X), sV)
        P.wait(ve, sV)
        P.inc(ve.tensor_single_scalar(gsc2["q"][:, 0:8], gsc2["p"][:, 0:8],
                                      m1[:, 5:6], ALU.is_ge), sV)
        P.inc(ve.tensor_tensor(m1[:, 6:7], m1[:, 4:5], m1[:, 5:6], ALU.add), sV)
        P.wait(ve, sV)
        P.inc(ve.reciprocal(m1[:, 7:8], m1[:, 6:7]), sV)
        P.wait(ve, sV)
        P.inc(ve.tensor_tensor(gsc2["q"][:, 0:8], gsc2["q"][:, 0:8],
                               gsc2["p"][:, 0:8], ALU.mult), sV)
        P.wait(ve, sV)
        P.inc(ve.tensor_scalar_mul(gsc2["q"][:, 0:8], gsc2["q"][:, 0:8],
                                   m1[:, 7:8]), sV)
        P.wait(ve, sV)
        P.inc(ve.tensor_copy(g16[:, st:st + 1], gsc2["q"][:, e_idx:e_idx + 1]), sV)
    P.wait(P.ge, sV)
    for st in range(4):
        P.dma(P.ge, gate_row[0:1, st * PB:(st + 1) * PB], g16[:, st:st + 1], sT)
    P.wait(ve, sT)


def build_l0d():
    P = Prog("l0d")
    xn_h = P.din("xn", [D, TE_ALL], BF16)
    lgt_h = P.din("lgt", [TE_ALL // 512, PB, 4, 8], F32)
    wu_h = P.din("wu", [32, PB, NKC, PB], F32)
    bu_h = P.din("bu", [PB, 32], F32)
    wd_h = P.din("wd", [NKC, PB, 32, PB], F32)
    bd_h = P.din("bd", [PB, NKC], F32)
    i128_h = P.din("i128", [PB, PB], BF16)
    eidx_h = P.din("eidx", [1, 8], I32)
    part_h = P.dout("part", [D, TE_ALL], BF16)
    P.begin()
    pool = PsumPool(P)
    sIn = P.sem("in")
    sSt = P.sem("st")
    gsc = P.sb("gsc", [PB, 512], F32)
    stage_sh = P.sb("wstage", [PB, 32 * PB], F32)
    wu = WResident(P, "wu", wu_h, 32, NKC, stage=stage_sh)
    wd = WResident(P, "wd", wd_h, NKC, 32, stage=stage_sh, stage_single=True)
    xt = FmBuf(P, "xt", NKC, 512, BF16)
    hm = FmBuf(P, "hmb", 32, 512, BF16)
    ob = FmBuf(P, "obb", NKC, 512, BF16)
    lgt = P.sb("lgts", [PB, 32], F32)
    grow = P.sb("grow", [1, 512], F32)
    g16r = P.sb("g16r", [1, 512], BF16)
    gb = P.sb("gbc", [PB, 512], F32)
    ident = P.sb("ident", [PB, PB], BF16)
    biasu = P.sb("biasu", [PB, 32], F32)
    biasd = P.sb("biasd", [PB, NKC], F32)
    gsc2 = {
        "g16": P.sb("g16", [PB, 4], F32),
        "m1": P.sb("m1", [PB, 8], F32),
        "p": P.sb("pp", [PB, 8], F32),
        "q": P.sb("qq", [PB, 8], F32),
    }
    P.dma(P.qe, ident[:, :], i128_h, sIn)
    P.dma(P.qe, biasu[:, :], bu_h, sIn)
    P.dma(P.qe, biasd[:, :], bd_h, sIn)
    wu.preload()
    P.barrier()
    wd.preload()
    EIDX = 0
    ntile = TE_ALL // 512
    sV = P.sem("misc")
    for e in (P.te, P.ve, P.se):
        P.wait(e, sIn)
    for it in range(ntile):
        t0 = it * 512
        sX = P.sem("xl")
        load_fm(P, xt, xn_h[:, t0:t0 + 512], sX)
        src = lgt_h[it].rearrange("p s e -> p (s e)")
        P.dma(P.qe, lgt[:, :], src, sX)
        for e in (P.te, P.ve, P.se):
            P.wait(e, sX)
        emit_gate(P, lgt, grow, 0, pool, gsc2, ident, pref="gt")
        P.wait(P.ve, P.sem("gtv"))
        P.inc(P.ve.tensor_copy(g16r[:, :], grow[:, :]), sV)
        P.wait(P.te, sV)
        pool.wait_free(P.te, 6)
        i = P.te.matmul(pool.banks[6][:, 0:512], gsc2["g16"][0:1, 0:1], g16r[:, :],
                        start=True, stop=True) if False else None
        if it == 0:
            ones1b = P.sb("ones1bd", [1, PB], BF16)
            P.inc(P.ve.memset(ones1b[:, :], 1.0), sV)
            P.wait(P.te, sV)
        i = P.te.matmul(pool.banks[6][:, 0:512], ones1b[0:1, :], g16r[:, :],
                        start=True, stop=True)
        P.inc(i, P.sem("gbt"))
        P.wait(P.ve, P.sem("gbt"))
        P.inc(P.ve.tensor_copy(gb[:, :], pool.banks[6][:, 0:512]), sV)
        pool.release(6, sV)
        emit_gemm(P, wu, xt, 512, xout=hm, bias=biasu, act=AF.Gelu_apprx_tanh,
                  sem_pref="u", pool=pool, banks=(0, 1))
        def post(mc, tt, bank, a0, a1, sTl, sEl, it=it):
            n = a1 - a0
            P.wait(P.se, sTl)
            P.inc(P.se.activation(gsc[:, 0:n], bank[:, 0:n], AF.Identity,
                                  bias=biasd[:, mc:mc + 1]), sEl)
            P.wait(P.ve, sEl)
            P.wait(P.ve, sV)
            P.inc(P.ve.tensor_tensor(ob.ch(mc, a0, a1), gsc[:, 0:n], gb[:, 0:512],
                                     ALU.mult), sEl)
        emit_gemm(P, wd, hm, 512, sem_pref="d", pool=pool, banks=(2, 3),
                  post=post)
        store_fm(P, ob, part_h[:, t0:t0 + 512], sSt,
                 wait_sems=[P.sem("de")])
        P.wait(P.qe, sSt)
        P.wait(P.ve, sSt)
    P.end()
    return P


def stage_l0d(inputs, core, xn_full_fm, lgt_tm):
    e = core
    Wfc = np.asarray(inputs["L0_ffn_Wfc"][e], np.float32)
    bfc = np.asarray(inputs["L0_ffn_bfc"][e], np.float32)
    Wpj = np.asarray(inputs["L0_ffn_Wproj"][e], np.float32)
    bpj = np.asarray(inputs["L0_ffn_bproj"][e], np.float32)
    lg = np.ascontiguousarray(np.roll(lgt_tm, -e, axis=1))
    lg4 = lg.reshape(TE_ALL // 512, 4, PB, 8).transpose(0, 2, 1, 3)
    return {
        "xn": xn_full_fm, "lgt": np.ascontiguousarray(lg4),
        "wu": pack_w(Wfc), "bu": bfc.reshape(32, PB).T.copy(),
        "wd": pack_w(Wpj), "bd": bpj.reshape(NKC, PB).T.copy(),
        "i128": np.eye(PB, dtype=BF), "eidx": np.zeros((1, 8), np.int32),
    }




def build_l0e():
    P = Prog("l0e")
    x_h = P.din("x", [D, TC], F32)
    s_h = P.din("s", [D, TC], BF16)
    sh_h = P.din("sh", [D, TC], BF16)
    pr_h = P.din("pr", [8, D, TC], BF16)
    wq_h = P.din("wq", [NKC, PB, NKC, PB], F32)
    bq_h = P.din("bq", [PB, NKC], F32)
    wk_h = P.din("wk", [NKC, PB, NKC, PB], F32)
    bk_h = P.din("bk", [PB, NKC], F32)
    wv_h = P.din("wv", [NKC, PB, NKC, PB], F32)
    bv_h = P.din("bv", [PB, NKC], F32)
    xo_h = P.dout("xo", [D, TC], F32)
    xn1_h = P.dout("xn1", [D, TC], BF16)
    o_h = {nm: P.dout(nm + "f", [D, TC], BF16) for nm in ("q", "k", "v")}
    P.begin()
    C = Common(P, TC)
    pool = C.pool
    sIn = P.sem("in")
    sSt = P.sem("st")
    x = FmBuf(P, "xs", NKC, TC, F32)
    acc = FmBuf(P, "accb", NKC, TC, F32)
    ob = FmBuf(P, "obb", NKC, TC, BF16)
    tmp = FmBuf(None, None, NKC, TC, BF16, tensor=ob.t)
    sld = FmBuf(None, None, NKC, TC, BF16, tensor=acc.t, col0=0)
    sld._bc_f32 = True
    xn1 = FmBuf(None, None, NKC, TC, BF16, tensor=acc.t, col0=NKC * TC)
    xn1._bc_f32 = True
    bias = P.sb("biasx", [PB, NKC], F32)
    esc = {"e1": P.sb("esc1", [PB, 512], F32), "e2": P.sb("esc2", [PB, 512], F32)}
    load_fm(P, x, x_h, sIn)
    sV = P.sem("mv")
    for e in (P.te, P.ve, P.se):
        P.wait(e, sIn)
    sIn2 = P.sem("in2")
    load_fm(P, tmp, sh_h, sIn2)
    P.wait(P.ve, sIn2)
    for kc in range(NKC):
        P.inc(P.ve.tensor_copy(acc.ch(kc), tmp.ch(kc)), sV)
    for e8 in range(8):
        sP = P.sem("pl")
        P.wait(P.qe, sV)
        load_fm(P, tmp, pr_h[e8], sP)
        P.wait(P.ve, sP)
        for kc in range(NKC):
            P.inc(P.ve.tensor_tensor(acc.ch(kc), acc.ch(kc), tmp.ch(kc),
                                     ALU.add), sV)
    P.wait(P.se, sV)
    sS = P.sem("ms")
    for kc in range(NKC):
        P.inc(P.se.activation(tmp.ch(kc), acc.ch(kc), AF.Silu), sS)
    P.barrier()
    sB = P.sem("sb")
    load_fm(P, sld, s_h, sB)
    P.wait(P.ve, sS)
    P.wait(P.ve, sB)
    emit_combine(P, x, sld, tmp, pref="cmb", scratch=C.gsc)
    store_fm(P, x, xo_h, sSt, wait_sems=[P.sem("cmbv")])
    P.barrier()
    emit_ln(P, x, xn1, TC, pref="ln1", lns=C.lns, pool=pool)
    store_fm(P, xn1, xn1_h, sSt, wait_sems=[P.sem("ln1v")])
    ws = WStream(P, "w", wq_h, NKC, NKC)
    for nm, wh, bh in (("q", wq_h, bq_h), ("k", wk_h, bk_h), ("v", wv_h, bv_h)):
        ws.w = wh
        P.barrier()
        sBn = P.sem(f"inb_{nm}")
        P.dma(P.qe, bias[:, :], bh, sBn)
        P.wait(P.se, sBn)
        emit_gemm(P, ws, xn1, TC, xout=ob, bias=bias, sem_pref=f"g{nm}",
                  pool=pool)
        if nm in ("q", "k"):
            P.wait(P.ve, P.sem(f"g{nm}e"))
            P.wait(P.se, P.sem(f"g{nm}e"))
            emit_elu1(P, ob, TC, esc, pref=f"el{nm}")
            P.wait(P.ge, P.sem(f"el{nm}v"))
        else:
            P.wait(P.ge, P.sem(f"g{nm}e"))
        store_fm(P, ob, o_h[nm], sSt)
    P.end()
    return P


def stage_l0e(inputs, core, s_fm, sh_fm, partials, x_fm):
    b, half = core // 2, core % 2
    d = {"x": x_fm, "s": s_fm, "sh": sh_fm, "pr": partials}
    for nm in ("q", "k", "v"):
        W = np.asarray(inputs[f"L1_attn_W{nm}"], np.float32)
        bb = np.asarray(inputs[f"L1_attn_b{nm}"], np.float32)
        d["w" + nm] = pack_w(W.T)
        d["b" + nm] = bb.reshape(NKC, PB).T.copy()
    return d



def build_l1b():
    P = Prog("l1b")
    ktm_h = P.din("ktm", [PB, TC // PB, D], BF16)
    vtm_h = P.din("vtm", [PB, TC // PB, D], BF16)
    xn1_h = P.din("xn1", [D, TC], BF16)
    w1_h = P.din("w1", [32, PB, NKC, PB], F32)
    s1_h = P.din("s1s", [PB, 32], F32)
    w2_h = P.din("w2", [32, PB, NKC, PB], F32)
    s2_h = P.din("s2s", [PB, 32], F32)
    h1_h = P.din("h1", [32, PB, 32, PB], F32)
    hs1_h = P.din("hs1", [PB, 32], F32)
    h2_h = P.din("h2", [32, PB, 32, PB], F32)
    hs2_h = P.din("hs2", [PB, 32], F32)
    w3_h = P.din("w3", [NKC, PB, 32, PB], F32)
    s3_h = P.din("s3s", [PB, NKC], F32)
    kv_h = P.dout("kvp", [64, 16, 65], F32)
    g1_h = P.dout("g1", [D, TC], BF16)
    P.begin()
    C = Common(P, TC)
    pool = C.pool
    sIn = P.sem("in")
    sSt = P.sem("st")
    ktm = P.sb("ktm_s", [PB, (TC // PB) * D], BF16)
    vtm = P.sb("vtm_s", [PB, (TC // PB) * D], BF16)
    onesc = P.sb("onescl", [PB, 1], BF16)
    kvsb = P.sb("kvsb", [64, 16 * 65], F32)
    P.dma(P.qe, ktm[:, :], ktm_h.rearrange("p t d -> p (t d)"), sIn)
    P.dma(P.qe, vtm[:, :], vtm_h.rearrange("p t d -> p (t d)"), sIn)
    P.inc(P.ve.memset(onesc[:, :], 1.0), P.sem("ini"))
    sT = P.sem("kvt")
    sV = P.sem("kvv")
    P.wait(P.te, sIn)
    P.wait(P.te, P.sem("ini"))
    ntb = TC // PB
    for h in range(16):
        bank = pool.banks[h % 2]
        pool.wait_free(P.te, h % 2)
        for tb in range(ntb):
            kap = ktm[:, tb * D + h * 64: tb * D + h * 64 + 64]
            vap = vtm[:, tb * D + h * 64: tb * D + h * 64 + 64]
            P.te.matmul(bank[0:64, 0:64], kap, vap, start=(tb == 0),
                        stop=(tb == ntb - 1))
            i = P.te.matmul(bank[0:64, 64:65], kap, onesc[:, :],
                            start=(tb == 0), stop=(tb == ntb - 1))
        P.inc(i, sT)
        P.wait(P.ve, sT)
        P.inc(P.ve.tensor_copy(kvsb[:, h * 65:(h + 1) * 65], bank[0:64, 0:65]), sV)
        pool.release(h % 2, sV)
    P.wait(P.ge, sV)
    P.dma(P.ge, kv_h.rearrange("d h c -> d (h c)"), kvsb[:, :], sSt)
    xn1 = FmBuf(P, "xn1b", NKC, TC, BF16)
    g1b = FmBuf(None, None, 32, 512, BF16, tensor=ktm)
    g2b = FmBuf(None, None, 32, 512, BF16, tensor=vtm)
    ob = FmBuf(P, "obb", NKC, 512, BF16)
    sIn2 = P.sem("in2")
    load_fm(P, xn1, xn1_h, sIn2)
    scl = P.sb("scl", [PB, 32], F32)
    for e in (P.te, P.ve, P.se):
        P.wait(e, sIn2)
    ws = WStream(P, "w", w1_h, 32, NKC)
    wsb = WStream(P, "wbig", h1_h, 32, 32, single=True)
    P.barrier()
    for tt in range(TC // 512):
        t0 = tt * 512
        sB = P.sem("sbt")
        P.dma(P.qe, scl[:, :], s1_h, sB)
        P.wait(P.se, sB)
        ws.w, ws.nmc, ws.nkc = w1_h, 32, NKC
        emit_gemm_scaled(P, ws, TileView(xn1, t0, 512), 512, g1b, scl,
                         AF.Silu, "gw1", pool, C.gsc)
        P.barrier()
        sB = P.sem("sbt")
        P.dma(P.qe, scl[:, :], s2_h, sB)
        P.wait(P.se, sB)
        ws.w = w2_h
        emit_gemm_scaled(P, ws, TileView(xn1, t0, 512), 512, g2b, scl,
                         AF.Identity, "gw2", pool, C.gsc)
        P.barrier()
        emit_tt3(P, g1b, g1b, g2b, ALU.mult, pref="hm1")
        P.barrier()
        sB = P.sem("sbt")
        P.dma(P.qe, scl[:, :], hs1_h, sB)
        P.wait(P.se, sB)
        wsb.w, wsb.nmc, wsb.nkc = h1_h, 32, 32
        emit_gemm_scaled(P, wsb, g1b, 512, g2b, scl, AF.Gelu, "gh1",
                         pool, C.gsc)
        P.barrier()
        sB = P.sem("sbt")
        P.dma(P.qe, scl[:, :], hs2_h, sB)
        P.wait(P.se, sB)
        wsb.w = h2_h
        emit_gemm_scaled(P, wsb, g2b, 512, g1b, scl, AF.Gelu, "gh2",
                         pool, C.gsc)
        P.barrier()
        sB = P.sem("sbt")
        P.dma(P.qe, scl[:, 0:NKC], s3_h, sB)
        P.wait(P.se, sB)
        wsb.w, wsb.nmc, wsb.nkc = w3_h, NKC, 32
        emit_gemm_scaled(P, wsb, g1b, 512, ob, scl, AF.Silu, "gw3",
                         pool, C.gsc, nmc=NKC)
        store_fm(P, ob, g1_h[:, t0:t0 + 512], sSt,
                 wait_sems=[P.sem("gw3e")])
        P.barrier()
    P.end()
    return P


def emit_gemm_scaled(P, ws, xin, T, xout, scl, fn, pref, pool, gsc, nmc=None):
    def post(mc, tt, bank, a0, a1, sTl, sEl):
        n = a1 - a0
        P.wait(P.se, sTl)
        P.inc(P.se.activation(xout.ch(mc, a0, a1), bank[:, 0:n], fn,
                              scale=scl[:, mc:mc + 1]), sEl)
    emit_gemm(P, ws, xin, T, sem_pref=pref, pool=pool, post=post, nmc=nmc)




def build_l1c():
    P = Prog("l1c")
    kvp_h = P.din("kvp", [64, 2, 16, 65], F32)
    qf_h = P.din("qf", [D, TC], BF16)
    g1_h = P.din("g1", [D, TC], BF16)
    x_h = P.din("x", [D, TC], F32)
    wo_h = P.din("wo", [NKC, PB, NKC, PB], F32)
    bo_h = P.din("bo", [PB, NKC], F32)
    xo_h = P.dout("xo", [D, TC], F32)
    P.begin()
    C = Common(P, TC)
    pool = C.pool
    sIn = P.sem("in")
    sSt = P.sem("st")
    sV = P.sem("mv")
    sT = P.sem("mt")
    sS = P.sem("ms")
    kv2 = P.sb("kv2", [64, 2 * 16 * 65], F32)
    kvb = P.sb("kvb", [64, 16 * 65], BF16)
    qf = FmBuf(P, "qfb", NKC, TC, BF16)
    ob = FmBuf(P, "obb", NKC, TC, BF16)
    sb = FmBuf(None, None, NKC, TC, BF16, tensor=qf.t)
    x = FmBuf(P, "xs", NKC, TC, F32)
    g1 = FmBuf(P, "g1b", NKC, TC, BF16)
    rrow = P.sb("rrow", [1, 512], F32)
    r16 = P.sb("r16", [1, 512], BF16)
    osb = P.sb("osb", [64, 512], BF16)
    ones1b = P.sb("ones1b", [1, PB], BF16)
    bias = P.sb("biasx", [PB, NKC], F32)
    P.dma(P.qe, kv2[:, :], kvp_h.rearrange("d two h c -> d (two h c)"), sIn)
    load_fm(P, qf, qf_h, sIn)
    load_fm(P, x, x_h, sIn)
    load_fm(P, g1, g1_h, sIn)
    P.dma(P.qe, bias[:, :], bo_h, sIn)
    P.inc(P.ve.memset(ones1b[:, :], 1.0), P.sem("ini"))
    for e in (P.te, P.ve, P.se):
        P.wait(e, sIn)
        P.wait(e, P.sem("ini"))
    P.inc(P.ve.tensor_tensor(kv2[:, 0:16 * 65], kv2[:, 0:16 * 65],
                             kv2[:, 16 * 65:2 * 16 * 65], ALU.add), sV)
    P.wait(P.ve, sV)
    P.inc(P.ve.tensor_copy(kvb[:, :], kv2[:, 0:16 * 65]), sV)
    P.wait(P.te, sV)
    qh = P.sb("qh1c", [64, 512], BF16)
    for h in range(16):
        qap_kc = h // 2
        r0 = (h % 2) * 64
        for tt in range(TC // 512):
            t0 = tt * 512
            bi = (h * 4 + tt) % 2
            bank = pool.banks[bi]
            pool.wait_free(P.te, bi)
            P.wait(P.ve, sT)
            P.inc(P.ve.tensor_copy(
                qh[:, :],
                qf.t[r0:r0 + 64, qap_kc * TC + t0: qap_kc * TC + t0 + 512]), sV)
            P.wait(P.te, sV)
            i = P.te.matmul(bank[0:65, 0:512], kvb[:, h * 65:(h + 1) * 65],
                            qh[:, :], start=True, stop=True)
            P.inc(i, sT)
            P.wait(P.ve, sT)
            P.inc(P.ve.tensor_scalar_add(rrow[:, :], bank[64:65, 0:512], 1e-6), sV)
            P.wait(P.ve, sV)
            P.inc(P.ve.reciprocal(rrow[:, :], rrow[:, :]), sV)
            P.wait(P.ve, sV)
            P.inc(P.ve.tensor_copy(r16[:, :], rrow[:, :]), sV)
            P.wait(P.se, sT)
            P.inc(P.se.activation(osb[:, :], bank[0:64, 0:512], AF.Copy), sS)
            bi2 = 2 + (h % 2)
            pool.wait_free(P.te, bi2)
            P.wait(P.te, sV)
            ib = P.te.matmul(pool.banks[bi2][0:64, 0:512], ones1b[0:1, 0:64],
                             r16[:, :], start=True, stop=True)
            P.inc(ib, sT)
            pool.release(bi, sS)
            P.wait(P.ve, sT)
            P.wait(P.ve, sS)
            P.inc(P.ve.tensor_tensor(
                ob.t[r0:r0 + 64, qap_kc * TC + t0: qap_kc * TC + t0 + 512],
                osb[:, :], pool.banks[bi2][0:64, 0:512], ALU.mult), sV)
            pool.release(bi2, sV)
    P.barrier()
    ws = WStream(P, "w", wo_h, NKC, NKC)
    emit_gemm(P, ws, ob, TC, xout=sb, bias=bias, sem_pref="gwo", pool=pool)
    P.wait(P.ve, P.sem("gwoe"))
    emit_combine(P, x, sb, g1, pref="cmb", scratch=C.gsc)
    store_fm(P, x, xo_h, sSt, wait_sems=[P.sem("cmbv")])
    P.end()
    return P



def build_l1d():
    P = Prog("l1d")
    x_h = P.din("x", [D, TC], F32)
    wq_h = P.din("wq", [24, PB, NKC, PB], F32)
    bq_h = P.din("bq", [PB, 24], F32)
    xn_h = P.dout("xn", [D, TC], BF16)
    qkv_h = P.dout("qkv", [3 * D, TC], BF16)
    P.begin()
    C = Common(P, TC)
    pool = C.pool
    sIn = P.sem("in")
    sSt = P.sem("st")
    x = FmBuf(P, "xs", NKC, TC, F32)
    xn = FmBuf(P, "xnb", NKC, TC, BF16)
    ring = FmBuf(P, "ring", 2, TC, BF16)
    bias = P.sb("biasx", [PB, 24], F32)
    load_fm(P, x, x_h, sIn)
    P.dma(P.qe, bias[:, :], bq_h, sIn)
    for e in (P.te, P.ve, P.se):
        P.wait(e, sIn)
    emit_ln(P, x, xn, TC, pref="ln", lns=C.lns, pool=pool)
    store_fm(P, xn, xn_h, sSt, wait_sems=[P.sem("lnv")])
    ws = WStream(P, "w", wq_h, 24, NKC)
    sE2 = P.sem("gqe")

    def post(mc, tt, bank, a0, a1, sTl, sEl):
        n = a1 - a0
        slot = mc % 2
        P.wait(P.se, sTl)
        if mc >= 2 and tt == 0:
            P.wait(P.se, sSt)
        P.inc(P.se.activation(ring.ch(slot, a0, a1), bank[:, 0:n], AF.Identity,
                              bias=bias[:, mc:mc + 1]), sEl)
        P.wait(P.ge, sEl)
        P.dma(P.ge, qkv_h[mc * PB:(mc + 1) * PB, a0:a1], ring.ch(slot, a0, a1), sSt)
    emit_gemm(P, ws, xn, TC, sem_pref="gq", pool=pool, post=post)
    P.end()
    return P


def stage_l1d(inputs, x_fm):
    W = np.asarray(inputs["L2_attn_qkv_w"], np.float32)
    bb = np.asarray(inputs["L2_attn_qkv_b"], np.float32)
    return {"x": x_fm, "wq": pack_w(W.T), "bq": bb.reshape(24, PB).T.copy()}




def build_l2a():
    P = Prog("l2a")
    q_h = P.din("q", [8, 64, S], BF16)
    k_h = P.din("k", [8, 64, S], BF16)
    va_h = P.din("va", [8, PB, 32, 65], BF16)
    al_h = P.din("al", [1, 8], F32)
    iota_h = P.din("iota", [1, S], F32)
    ctx_h = P.dout("ctx", [8, 64, S], BF16)
    P.begin()
    pool = PsumPool(P)
    sc = attn_scratch(P, minf=True)
    sc2 = hb_scratch(P)
    sIn = P.sem("in")
    sSt = P.sem("st")
    sV = P.sem("mv")
    sS = P.sem("ms")
    NR = 70
    QA = P.sb("QA", [NR, S], BF16)
    QB = P.sb("QB", [NR, S], BF16)
    KA = P.sb("KA", [NR, S], BF16)
    KB = P.sb("KB", [NR, S], BF16)
    VA = P.sb("VAs", [PB, 32 * 65], BF16)
    CX = P.sb("CX", [64, S], BF16)
    cb = P.sb("cb", [PB, 1], F32)
    iota = P.sb("iota_s", [1, S], F32)
    alr = P.sb("alr", [1, 8], F32)
    spr = P.sb("spr", [1, 16], F32)
    rowf = P.sb("rowf", [1, S], F32)
    rowf2 = P.sb("rowf2", [1, S], F32)
    rowb = P.sb("rowb", [1, S], BF16)
    P.dma(P.qe, iota[:, :], iota_h, sIn)
    P.dma(P.qe, alr[:, :], al_h, sIn)
    P.inc(P.ve.memset(sc["ones1b"][:, :], 1.0), P.sem("ini"))
    P.inc(P.ve.memset(sc2["onesb64"][:, :], 1.0), P.sem("ini"))
    P.inc(P.ve.memset(sc2["ones1b"][:, :], 1.0), P.sem("ini"))
    for T_ in (QA, QB, KA, KB):
        P.inc(P.ve.memset(T_[64:NR, :], 1.0), P.sem("ini"))
    for e in (P.te, P.ve, P.se):
        P.wait(e, sIn)
        P.wait(e, P.sem("ini"))
    P.inc(P.se.activation(spr[:, 0:8], alr[:, :], AF.Abs), sS)
    P.wait(P.se, sS)
    P.inc(P.se.activation(spr[:, 0:8], spr[:, 0:8], AF.Exp, scale=-1.0), sS)
    P.wait(P.se, sS)
    P.inc(P.se.activation(spr[:, 0:8], spr[:, 0:8], AF.Ln, bias=1.0), sS)
    P.inc(P.se.activation(spr[:, 8:16], alr[:, :], AF.Relu), sS)
    P.wait(P.ve, sS)
    P.inc(P.ve.tensor_tensor(alr[:, :], spr[:, 0:8], spr[:, 8:16], ALU.add), sV)
    P.wait(P.ve, sV)
    P.inc(P.ve.tensor_scalar_mul(alr[:, :], alr[:, :], 8.0), sV)
    sched = []
    for qt in range(8):
        kbs = []
        for kb in range(32):
            lo, hi = kb * PB, kb * PB + PB - 1
            if hi < qt * 512:
                kbs.append((kb, None, 0, None))
            elif lo > qt * 512 + 511:
                kbs.append((kb, None, 1, None))
            else:
                kbs.append((kb, None, 0, 1))
        sched.append((qt, kbs))
    for h in range(8):
        sH = P.sem("hh")
        P.wait(P.qe, sSt)
        P.dma(P.qe, QA[0:64, :], q_h[h], sH)
        P.dma(P.qe, QB[0:64, :], q_h[h], sH)
        P.dma(P.qe, KA[0:64, :], k_h[h], sH)
        P.dma(P.qe, KB[0:64, :], k_h[h], sH)
        P.dma(P.qe, VA[:, :], va_h[h].rearrange("p kb c -> p (kb c)"), sH)
        for e in (P.te, P.ve, P.se):
            P.wait(e, sH)
            P.wait(e, sV)
        sR = P.sem("rr")
        for (buf, sign, rows) in ((QA, -1.0, (64, 65, 66)), (QB, 1.0, (64, 65, 66)),
                                  (KA, 1.0, (67, 68, 69)), (KB, -1.0, (67, 68, 69))):
            P.inc(P.ve.tensor_single_scalar(rowf[:, :], iota[:, :],
                                            alr[:, h:h + 1], ALU.mult), sV)
            P.wait(P.ve, sV)
            if sign < 0:
                P.inc(P.ve.tensor_scalar_mul(rowf[:, :], rowf[:, :], -1.0), sV)
                P.wait(P.ve, sV)
            cur = rowf
            other = rowf2
            for ri in rows:
                P.wait(P.ve, sR)
                P.inc(P.ve.tensor_copy(rowb[:, :], cur[:, :]), sV)
                P.wait(P.ve, sV)
                P.inc(P.ve.tensor_tensor(other[:, :], cur[:, :], rowb[:, :],
                                         ALU.subtract), sV)
                P.wait(P.ge, sV)
                P.dma(P.ge, buf[ri:ri + 1, :], rowb[:, :], sR)
                cur, other = other, cur
            P.wait(P.ve, sR)
        emit_head_bound(P, QA[0:64, :], KA[0:64, :], S, S, cb[:, :], pool, sc2,
                        pref="hb")
        for e in (P.te, P.ve, P.se):
            P.wait(e, sR)
        hd = {
            "Q": [QA[:, :], QB[:, :]], "K": [KA[:, :], KB[:, :]], "VA": VA[:, :],
            "sched": sched, "cb": cb[:, :],
            "ctx": lambda a, b: CX[:, a:b], "sc": sc, "nr": NR,
        }
        emit_attn(P, [hd], S, pool, pref="at")
        P.wait(P.ge, P.sem("atv"))
        P.dma(P.ge, ctx_h[h], CX[:, :], sSt)
        P.wait(P.qe, sSt)
    P.end()
    return P


def stage_l2a(inputs, core, qkv_by_core):
    b, half = core // 2, core % 2
    qkv = np.concatenate([qkv_by_core[2 * b], qkv_by_core[2 * b + 1]], axis=1)
    hsl = slice(half * 8 * 64, (half + 1) * 8 * 64)
    q = qkv[0:D][hsl].reshape(8, 64, S)
    k = qkv[D:2 * D][hsl].reshape(8, 64, S)
    v = qkv[2 * D:3 * D][hsl].reshape(8, 64, S)
    va = np.concatenate([v.transpose(0, 2, 1), np.ones((8, S, 1), BF)], axis=2)
    va4 = np.ascontiguousarray(va.reshape(8, 32, PB, 65).transpose(0, 2, 1, 3))
    al = np.asarray(inputs["L2_attn_decay_alpha"], np.float32)[half * 8:(half + 1) * 8]
    return {"q": np.ascontiguousarray(q), "k": np.ascontiguousarray(k),
            "va": va4, "al": al.reshape(1, 8),
            "iota": np.arange(S, dtype=np.float32).reshape(1, S)}




TST = 3584
TQA = 2560
HID2 = 2816

def build_inner(tag, combine=False):
    P = Prog("inner" + tag)
    xh_h = P.din("xh", [D, TST], BF16)
    if combine:
        xr_h = P.din("xr", [D, TC], F32)
        sr_h = P.din("sr", [D, TC], BF16)
        xof_h = P.dout("xof", [D, TC], F32)
        xnn_h = P.dout("xnn", [D, TC], BF16)
    wqkv_h = P.din("wqkv", [12, PB, NKC, PB], F32)
    qs_h = P.din("qs", [64, 1], F32)
    ks_h = P.din("ks", [64, 1], F32)
    czq_h = P.din("czq", [64, TST], F32)
    szq_h = P.din("szq", [64, TST], F32)
    czk_h = P.din("czk", [64, TST], F32)
    szk_h = P.din("szk", [64, TST], F32)
    mk_h = P.din("mk", [PB, 8, 512], BF16)
    mk0_h = P.din("mk0", [PB, 4, 512], BF16)
    i128_h = P.din("i128", [PB, PB], BF16)
    wo_h = P.din("wo", [NKC, PB, 4, PB], F32)
    lg_h = P.din("lg", [PB, NKC], F32)
    lb_h = P.din("lb", [PB, NKC], F32)
    wa_h = P.din("wa", [22, PB, NKC, PB], F32)
    ba_h = P.din("ba", [PB, 22], F32)
    wg_h = P.din("wg", [22, PB, NKC, PB], F32)
    bg_h = P.din("bg", [PB, 22], F32)
    w2_h = P.din("w2", [NKC, PB, 22, PB], F32)
    b2_h = P.din("b2", [PB, NKC], F32)
    first_h = P.din("first", [1, 8], F32)
    xo_h = P.dout("xo", [D, TC], BF16)
    P.begin()
    C = Common(P, 512)
    pool = C.pool
    sc = attn_scratch(P)
    sIn = P.sem("in")
    sV = P.sem("mv")
    sS = P.sem("ms")
    sT = P.sem("mt")
    xf1_d = P.dtmp("xf1d" + tag, [D, TQA], BF16)
    xin = FmBuf(P, "xin", NKC, 513, BF16)
    qkvb = FmBuf(P, "qkvb", 12, 512, BF16)
    KH = [P.sb(f"KH{h}", [65, 1024], BF16) for h in range(8)]
    VAr = [P.sb(f"VAr{h}", [PB, 8 * 65], BF16) for h in range(8)]
    KNm = P.sb("KNm", [1, 16], F32)
    KR = P.sb("KRx", [1, 512], F32)
    QH = P.sb("QHx", [65, 512], BF16)
    QN = P.sb("QNx", [1, 512], F32)
    wrow = P.sb("wrow", [1, 512], F32)
    wb16 = P.sb("wb16", [1, 512], BF16)
    wm1 = P.sb("wm1", [1, 4], F32)
    t1 = P.sb("t1x", [64, 512], F32)
    t2 = P.sb("t2x", [64, 512], F32)
    sqb = P.sb("sqbx", [64, 512], BF16)
    tabc = P.sb("tabc", [64, 512], F32)
    tabs = P.sb("tabs", [64, 512], F32)
    qsv = P.sb("qsvx", [64, 2], F32)
    o64 = P.sb("o64x", [64, 1], BF16)
    on1 = P.sb("on1x", [1, PB], BF16)
    ident = P.sb("identx", [PB, PB], BF16)
    MK = P.sb("MKxi", [PB, 8 * 512], BF16)
    MK0 = P.sb("MK0xi", [PB, 4 * 512], BF16)
    OW = FmBuf(P, "oww", 4, 512, BF16)
    XT = xin
    XLN = FmBuf(P, "xlnw", NKC, 512, BF16)
    HA = FmBuf(P, "haw", 22, 512, BF16)
    HG = FmBuf(P, "hgw", 22, 512, BF16)
    OB = FmBuf(P, "obw", NKC, 512, BF16)
    gam = P.sb("gamx", [PB, NKC], F32)
    bet = P.sb("betx", [PB, NKC], F32)
    bias22 = P.sb("bias22", [PB, 22], F32)
    bias8 = P.sb("bias8", [PB, NKC], F32)
    if combine:
        xrt = FmBuf(P, "xrt", NKC, 512, F32)
        srt = FmBuf(None, None, NKC, 512, BF16, tensor=qkvb.t)
        xnn = FmBuf(None, None, NKC, 512, BF16, tensor=XLN.t)
    P.dma(P.qe, ident[:, :], i128_h, sIn)
    P.dma(P.qe, qsv[:, 0:1], qs_h, sIn)
    P.dma(P.qe, qsv[:, 1:2], ks_h, sIn)
    P.dma(P.qe, gam[:, :], lg_h, sIn)
    P.dma(P.qe, bet[:, :], lb_h, sIn)
    P.dma(P.qe, MK[:, :], mk_h.rearrange("p m c -> p (m c)"), sIn)
    P.dma(P.qe, MK0[:, :], mk0_h.rearrange("p m c -> p (m c)"), sIn)
    sini = P.sem("ini")
    P.inc(P.ve.memset(sc["ones1b"][:, :], 1.0), sini)
    P.inc(P.ve.memset(o64[:, :], 1.0), sini)
    P.inc(P.ve.memset(on1[:, :], 1.0), sini)
    for h in range(8):
        P.inc(P.ve.memset(KH[h][64:65, :], 1.0), sini)
    for e in (P.te, P.ve, P.se, P.ge):
        P.wait(e, sIn)
        P.wait(e, sini)
    ws = WStream(P, "w", wqkv_h, 12, NKC)
    wsb = WStream(P, "wbig", w2_h, NKC, 22, single=True)

    def nxpos(src, scol, dst_rows, norm_row):
        b7 = pool.banks[7]
        P.wait(P.se, sT)
        P.inc(P.se.activation(sqb[:, :], src, AF.Square), sS)
        pool.wait_free(P.te, 7)
        P.wait(P.te, sS)
        P.inc(P.te.matmul(b7[0:1, 0:512], o64[:, :], sqb[:, :],
                          start=True, stop=True), sT)
        P.wait(P.ve, sT)
        P.inc(P.ve.tensor_copy(wrow[:, :], b7[0:1, 0:512]), sV)
        pool.release(7, sV)
        P.wait(P.se, sV)
        P.inc(P.se.activation(wrow[:, :], wrow[:, :], AF.Sqrt), sS)
        P.wait(P.ve, sS)
        P.inc(P.ve.tensor_scalar_max(wrow[:, :], wrow[:, :], 1e-12), sV)
        P.wait(P.ve, sV)
        P.inc(P.ve.reciprocal(wrow[:, :], wrow[:, :]), sV)
        P.wait(P.ve, sV)
        P.inc(P.ve.tensor_copy(wb16[:, :], wrow[:, :]), sV)
        pool.wait_free(P.te, 7)
        P.wait(P.te, sV)
        P.inc(P.te.matmul(b7[0:64, 0:512], on1[0:1, 0:64], wb16[:, :],
                          start=True, stop=True), sT)
        P.wait(P.ve, sT)
        P.inc(P.ve.tensor_tensor(t1[:, :], src, b7[0:64, 0:512], ALU.mult), sV)
        pool.release(7, sV)
        P.wait(P.ve, sV)
        P.inc(P.ve.tensor_scalar_mul(t1[:, :], t1[:, :],
                                     qsv[:, scol:scol + 1]), sV)
        P.wait(P.ve, sV)
        P.inc(P.ve.tensor_scalar_mul(t2[0:32, :], t1[32:64, :], -1.0), sV)
        P.inc(P.ve.tensor_copy(t2[32:64, :], t1[0:32, :]), sV)
        P.wait(P.ve, sV)
        P.inc(P.ve.tensor_tensor(t1[:, :], t1[:, :], tabc[:, :], ALU.mult), sV)
        P.inc(P.ve.tensor_tensor(t2[:, :], t2[:, :], tabs[:, :], ALU.mult), sV)
        P.wait(P.ve, sV)
        P.inc(P.ve.tensor_tensor(t1[:, :], t1[:, :], t2[:, :], ALU.add), sV)
        P.wait(P.ve, sV)
        P.inc(P.ve.tensor_copy(dst_rows, t1[:, :]), sV)
        P.wait(P.se, sV)
        P.inc(P.se.activation(sqb[:, :], t1[:, :], AF.Square), sS)
        pool.wait_free(P.te, 7)
        P.wait(P.te, sS)
        P.inc(P.te.matmul(b7[0:1, 0:512], o64[:, :], sqb[:, :],
                          start=True, stop=True), sT)
        P.wait(P.ve, sT)
        P.inc(P.ve.tensor_copy(norm_row, b7[0:1, 0:512]), sV)
        pool.release(7, sV)
        P.wait(P.se, sV)
        P.inc(P.se.activation(norm_row, norm_row, AF.Sqrt), sS)
        P.wait(P.ve, sS)

    class ShiftT:
        nkc, T, dtype = NKC, 512, BF16
        def ch(self, kc, t0=0, t1=None):
            t1 = 512 if t1 is None else t1
            off = 1 if kc < 4 else 0
            return xin.ch(kc, t0 + off, t1 + off)

    class ResidW:
        nkc, T, dtype = NKC, 512, BF16
        def ch(self, kc, t0=0, t1=None):
            t1 = 512 if t1 is None else t1
            return xin.ch(kc, 1 + t0, 1 + t1)

    for b in range(6):
        st0 = 512 + 512 * b
        par = b % 2
        P.barrier()
        sL = P.sem("bl")
        for kc in range(NKC):
            P.dma(P.qe, xin.ch(kc), xh_h[kc * PB:(kc + 1) * PB,
                                         st0 - 1:st0 + 512], sL)
        for e in (P.te, P.ve, P.se):
            P.wait(e, sL)
        ws.w, ws.nmc, ws.nkc = wqkv_h, 12, NKC
        emit_gemm(P, ws, ShiftT(), 512, xout=qkvb, sem_pref="gq", pool=pool)
        P.barrier()
        sTb = P.sem("tb")
        P.dma(P.qe, tabc[:, :], czk_h[:, st0:st0 + 512], sTb)
        P.dma(P.qe, tabs[:, :], szk_h[:, st0:st0 + 512], sTb)
        for e in (P.te, P.ve, P.se):
            P.wait(e, sTb)
        for h in range(8):
            kc, r0 = (512 + h * 64) // PB, (h * 64) % PB
            src = qkvb.t[r0:r0 + 64, kc * 512:(kc + 1) * 512]
            nxpos(src, 1, KH[h][0:64, par * 512:par * 512 + 512],
                  KR[0:1, :])
            P.inc(P.ve.reduce_max(KNm[0:1, h * 2 + par:h * 2 + par + 1],
                                  KR[0:1, :], axis=AX.X), sV)
            P.wait(P.ve, sV)
            vkc, vr0 = (1024 + h * 64) // PB, (h * 64) % PB
            vap = qkvb.t[vr0:vr0 + 64, vkc * 512:(vkc + 1) * 512]
            emit_vaug(P, vap, VAr[h][:, par * 260:par * 260 + 260], 4, ident,
                      pool, pref="tv", r0=vr0)
            for e in (P.te, P.ve, P.se):
                P.wait(e, P.sem("tvv"))
        if b == 0:
            continue
        w = b - 1
        P.barrier()
        sTq = P.sem("tq")
        P.dma(P.qe, tabc[:, :], czq_h[:, st0:st0 + 512], sTq)
        P.dma(P.qe, tabs[:, :], szq_h[:, st0:st0 + 512], sTq)
        for e in (P.te, P.ve, P.se):
            P.wait(e, sTq)
        kbs = []
        for j in range(8):
            blk = w + (0 if j < 4 else 1)
            parb = blk % 2
            kb_ring = parb * 4 + (j % 4)
            mk_src = MK0 if (w <= 1 and j < 4) else MK
            kbs.append((kb_ring, mk_src[:, 512 * j:512 * j + 512], 0, None))
        for h in range(8):
            kc, r0 = (h * 64) // PB, (h * 64) % PB
            src = qkvb.t[r0:r0 + 64, kc * 512:(kc + 1) * 512]
            nxpos(src, 0, QH[0:64, :], QN[0:1, :])
            P.inc(P.ve.reduce_max(wm1[:, 0:1],
                                  KNm[0:1, h * 2:h * 2 + 2], axis=AX.X), sV)
            P.wait(P.ve, sV)
            P.inc(P.ve.tensor_single_scalar(QN[0:1, :], QN[0:1, :],
                                            wm1[:, 0:1], ALU.mult), sV)
            obk = pool.banks[6]
            for lam in (0.005, 0.02, 0.1, 0.5, 2.0):
                P.wait(P.ve, sV)
                P.inc(P.ve.tensor_scalar_mul(wrow[:, :], QN[0:1, :], -1.0), sV)
                P.wait(P.ve, sV)
                P.inc(P.ve.tensor_copy(wb16[:, :], wrow[:, :]), sV)
                P.wait(P.ge, sV)
                sQr = P.sem("qr")
                P.dma(P.ge, QH[64:65, :], wb16[:, :], sQr)
                for e in (P.te, P.ve, P.se):
                    P.wait(e, sQr)
                pool.wait_free(P.te, 6)
                iav = None
                for jj, (kb_ring, mask, _v, _m) in enumerate(kbs):
                    sbi = 4 + (jj % 2)
                    sbank = pool.banks[sbi]
                    pool.wait_free(P.te, sbi)
                    i = P.te.matmul(sbank[:, 0:512],
                                    KH[h][:, kb_ring * PB:(kb_ring + 1) * PB],
                                    QH[:, :], start=True, stop=True)
                    P.inc(i, sT)
                    P.wait(P.ve, sT)
                    P.inc(P.ve.tensor_tensor(sbank[:, 0:512], sbank[:, 0:512],
                                             mask, ALU.add), sV)
                    P.wait(P.se, sV)
                    P.wait(P.se, sT)
                    pt = sc["pt"][:, (jj % 2) * 512:(jj % 2) * 512 + 512]
                    P.inc(P.se.activation(pt, sbank[:, 0:512], AF.Exp,
                                          scale=lam), sS)
                    pool.release(sbi, sS)
                    P.wait(P.te, sS)
                    iav = P.te.matmul(
                        obk[0:1, 0:512],
                        VAr[h][:, kb_ring * 65 + 64:kb_ring * 65 + 65], pt,
                        start=(jj == 0), stop=(jj == len(kbs) - 1))
                P.inc(iav, sT)
                P.wait(P.ve, sT)
                P.inc(P.ve.tensor_copy(wrow[:, :], obk[0:1, 0:512]), sV)
                pool.release(6, sV)
                P.wait(P.se, sV)
                P.inc(P.se.activation(wrow[:, :], wrow[:, :], AF.Ln), sS)
                P.wait(P.ve, sS)
                P.inc(P.ve.tensor_scalar_mul(wrow[:, :], wrow[:, :],
                                             1.0 / lam), sV)
                P.wait(P.ve, sV)
                P.inc(P.ve.tensor_tensor(QN[0:1, :], QN[0:1, :], wrow[:, :],
                                         ALU.add), sV)
            P.wait(P.ve, sV)
            P.inc(P.ve.tensor_scalar_mul(wrow[:, :], QN[0:1, :], -1.0), sV)
            P.wait(P.ve, sV)
            P.inc(P.ve.tensor_copy(wb16[:, :], wrow[:, :]), sV)
            P.wait(P.ge, sV)
            sQr = P.sem("qr")
            P.dma(P.ge, QH[64:65, :], wb16[:, :], sQr)
            for e in (P.te, P.ve, P.se):
                P.wait(e, sQr)
            hd = {
                "Q": [QH[:, :]], "K": [KH[h][:, :]], "VA": VAr[h][:, :],
                "sched": [(0, kbs)], "cb": 0.0,
                "ctx": (lambda a, bb, h=h: OW.t[(h % 2) * 64:(h % 2) * 64 + 64,
                                                (h // 2) * 512 + a:
                                                (h // 2) * 512 + bb]),
                "sc": sc,
            }
            emit_attn(P, [hd], 512, pool, pref="at", exp_scale=8.0)
        P.barrier()
        ws.w, ws.nmc, ws.nkc = wo_h, NKC, 4
        emit_gemm(P, ws, OW, 512, xout=OB, resid=ResidW(),
                  sem_pref="gwo", pool=pool, scratch=C.gsc)
        sX = P.sem("xf")
        store_fm(P, OB, xf1_d[:, 512 * w:512 * w + 512], sX,
                 wait_sems=[P.sem("gwoe")])
        P.barrier()

    class ShiftF2:
        nkc, T, dtype = NKC, 512, BF16
        def ch(self, kc, t0=0, t1=None):
            t1 = 512 if t1 is None else t1
            off = 1 if kc < 4 else 0
            return XT.ch(kc, t0 + off, t1 + off)

    class ResidF2:
        nkc, T, dtype = NKC, 512, BF16
        def ch(self, kc, t0=0, t1=None):
            t1 = 512 if t1 is None else t1
            return XT.ch(kc, 1 + t0, 1 + t1)

    sSt = P.sem("stf")
    for t in range(4):
        i0 = 512 + 512 * t
        P.barrier()
        sL2 = P.sem("ft")
        for kc in range(NKC):
            P.dma(P.qe, XT.ch(kc), xf1_d[kc * PB:(kc + 1) * PB,
                                         i0 - 1:i0 + 512], sL2)
        if combine:
            load_fm(P, xrt, xr_h[:, 512 * t:512 * t + 512], sL2)
            load_fm(P, srt, sr_h[:, 512 * t:512 * t + 512], sL2)
        for e in (P.te, P.ve, P.se):
            P.wait(e, sL2)
        emit_ln(P, ShiftF2(), XLN, 512, gamma=gam, beta=bet, pref="lnf",
                lns=C.lns, pool=pool)
        P.barrier()
        sB = P.sem("bbt")
        P.dma(P.qe, bias22[:, :], ba_h, sB)
        P.wait(P.se, sB)
        ws.w, ws.nmc, ws.nkc = wa_h, 22, NKC
        emit_gemm(P, ws, XLN, 512, xout=HA, bias=bias22, sem_pref="gfa",
                  pool=pool)
        P.barrier()
        sB2 = P.sem("bbt")
        P.dma(P.qe, bias22[:, :], bg_h, sB2)
        P.wait(P.se, sB2)
        ws.w = wg_h
        emit_gemm(P, ws, XLN, 512, xout=HG, bias=bias22, act=AF.Gelu,
                  sem_pref="gfg", pool=pool)
        P.barrier()
        emit_tt3(P, HA, HA, HG, ALU.mult, pref="geg")
        P.barrier()
        sB3 = P.sem("bbt")
        P.dma(P.qe, bias8[:, :], b2_h, sB3)
        P.wait(P.se, sB3)
        wsb.w, wsb.nmc, wsb.nkc = w2_h, NKC, 22
        emit_gemm(P, wsb, HA, 512, xout=OB, bias=bias8, resid=ResidF2(),
                  sem_pref="gf2", pool=pool, scratch=C.gsc)
        P.barrier()
        store_fm(P, OB, xo_h[:, 512 * t:512 * t + 512], sSt)
        if combine:
            P.wait(P.se, sSt)
            sS2 = P.sem("cs")
            for kc in range(NKC):
                P.inc(P.se.activation(OB.ch(kc), OB.ch(kc), AF.Silu), sS2)
            P.wait(P.ve, sS2)
            emit_combine(P, xrt, srt, OB, pref="cmb", scratch=C.gsc)
            store_fm(P, xrt, xof_h[:, 512 * t:512 * t + 512], sSt,
                     wait_sems=[P.sem("cmbv")])
            P.barrier()
            emit_ln(P, xrt, xnn, 512, pref="lnn", lns=C.lns, pool=pool)
            store_fm(P, xnn, xnn_h[:, 512 * t:512 * t + 512], sSt,
                     wait_sems=[P.sem("lnnv")])
    P.end()
    return P


def make_xpos_tables(t0_abs):
    DIM_HEAD, WINDOW = 64, 512
    half_idx = np.arange(0, DIM_HEAD, 2, dtype=np.float32)
    inv_freq = 1.0 / 10000.0 ** (half_idx / DIM_HEAD)
    t = (np.arange(TST, dtype=np.float32) + t0_abs)
    freqs = t[:, None] * inv_freq[None, :]
    freqs = np.concatenate([freqs, freqs], axis=-1)
    base = (half_idx + 0.4 * DIM_HEAD) / (1.4 * DIM_HEAD)
    power = (t - S // 2) / float(WINDOW // 2)
    scale = base[None, :] ** power[:, None]
    scale = np.concatenate([scale, scale], axis=-1)
    cos, sin = np.cos(freqs), np.sin(freqs)
    return (np.ascontiguousarray((cos * scale).T), np.ascontiguousarray((sin * scale).T),
            np.ascontiguousarray((cos / scale).T), np.ascontiguousarray((sin / scale).T))


def stage_inner(inputs, core, lidx, pref, xh_fm):
    b, half = core // 2, core % 2
    own0 = half * TC
    wqkv = np.asarray(inputs[f"{pref}_wqkv"][lidx], np.float32)
    qs = np.asarray(inputs[f"{pref}_q_scale"][lidx], np.float32).reshape(64, 1)
    ks = np.asarray(inputs[f"{pref}_k_scale"][lidx], np.float32).reshape(64, 1)
    wo = np.asarray(inputs[f"{pref}_wo"][lidx], np.float32)
    lg = np.asarray(inputs[f"{pref}_ln_g"][lidx], np.float32)
    lb = np.asarray(inputs[f"{pref}_ln_b"][lidx], np.float32)
    w1 = np.asarray(inputs[f"{pref}_w1"][lidx], np.float32)
    b1 = np.asarray(inputs[f"{pref}_b1"][lidx], np.float32)
    w2 = np.asarray(inputs[f"{pref}_w2"][lidx], np.float32)
    b2 = np.asarray(inputs[f"{pref}_b2"][lidx], np.float32)
    HID = 2730
    wa = np.pad(w1[:, :HID], ((0, 0), (0, HID2 - HID)))
    wg = np.pad(w1[:, HID:], ((0, 0), (0, HID2 - HID)))
    ba = np.pad(b1[:HID], (0, HID2 - HID))
    bg = np.pad(b1[HID:], (0, HID2 - HID))
    w2p = np.pad(w2, ((0, HID2 - HID), (0, 0)))
    czq, szq, czk, szk = make_xpos_tables(own0 - 1536)
    mk = np.zeros((8, PB, 512), np.float32)
    for jb in range(8):
        ki = np.arange(PB)[:, None] + jb * PB - 512
        qi = np.arange(512)[None, :]
        bad = (ki > qi) | (ki <= qi - 512)
        mk[jb][bad] = NEG
    mk0 = mk.copy()
    first = 1.0 if half == 0 else 0.0
    if half == 0:
        for jb in range(4):
            mk0[jb][:, :] = NEG
    return {
        "xh": xh_fm, "wqkv": pack_w(wqkv), "qs": qs, "ks": ks,
        "czq": czq, "szq": szq, "czk": czk, "szk": szk,
        "mk": np.ascontiguousarray(mk.transpose(1, 0, 2)).astype(BF),
        "mk0": np.ascontiguousarray(mk0[:4].transpose(1, 0, 2)).astype(BF),
        "i128": np.eye(PB, dtype=BF),
        "wo": pack_w(wo), "lg": lg.reshape(NKC, PB).T.copy(),
        "lb": lb.reshape(NKC, PB).T.copy(),
        "wa": pack_w(wa), "ba": ba.reshape(22, PB).T.copy(),
        "wg": pack_w(wg), "bg": bg.reshape(22, PB).T.copy(),
        "w2": pack_w(w2p), "b2": b2.reshape(NKC, PB).T.copy(),
        "first": np.full((1, 8), first, np.float32),
    }




def build_l2b():
    P = Prog("l2b")
    ctx_h = P.din("ctx", [D, TC], BF16)
    wo_h = P.din("wo", [NKC, PB, NKC, PB], F32)
    bo_h = P.din("bo", [PB, NKC], F32)
    s_h = P.dout("s", [D, TC], BF16)
    P.begin()
    C = Common(P, TC)
    pool = C.pool
    sIn = P.sem("in")
    ctx = FmBuf(P, "ctxb", NKC, TC, BF16)
    ob = FmBuf(P, "obb", NKC, TC, BF16)
    bias = P.sb("biasx", [PB, NKC], F32)
    load_fm(P, ctx, ctx_h, sIn)
    P.dma(P.qe, bias[:, :], bo_h, sIn)
    for e in (P.te, P.ve, P.se):
        P.wait(e, sIn)
    ws = WStream(P, "w", wo_h, NKC, NKC)
    emit_gemm(P, ws, ctx, TC, xout=ob, bias=bias, sem_pref="g", pool=pool)
    store_fm(P, ob, s_h, P.sem("st"), wait_sems=[P.sem("ge")])
    P.end()
    return P



T3 = 2560

def build_l3a():
    P = Prog("l3a")
    xh_h = P.din("xh", [D, T3], BF16)
    wqkv_h = P.din("wqkv", [12, PB, NKC, PB], F32)
    cz_h = P.din("cz", [64, T3], F32)
    sz_h = P.din("sz", [64, T3], F32)
    mk_h = P.din("mk", [PB, 12, 512], F32)
    i128_h = P.din("i128", [PB, PB], BF16)
    wout_h = P.din("wout", [NKC, PB, 4, PB], F32)
    s_h = P.dout("s", [D, TC], BF16)
    P.begin()
    C = Common(P, T3)
    pool = C.pool
    sc = attn_scratch(P)
    sc2 = hb_scratch(P)
    sIn = P.sem("in")
    sV = P.sem("mv")
    sS = P.sem("ms")
    xh = FmBuf(P, "xhb", NKC, T3, BF16)
    qkv_d = P.dtmp("qkv3d", [12 * PB, T3], BF16)
    ring = FmBuf(P, "ring3", 2, T3, BF16)
    QS = P.sb("QS", [64, T3], BF16)
    VH = P.sb("VH", [64, T3], BF16)
    QH = P.sb("QH", [64, T3], BF16)
    KH = P.sb("KH", [64, T3], BF16)
    VAW = P.sb("VAW", [PB, 20 * 65], BF16)
    OF = FmBuf(P, "ofb", 4, TC, BF16)
    MK = P.sb("MKx", [PB, 12 * 512], F32)
    ident = P.sb("ident", [PB, PB], BF16)
    cz = P.sb("czs", [64, T3], F32)
    sz = P.sb("szs", [64, T3], F32)
    cb = P.sb("cbx", [PB, 1], F32)
    t1 = P.sb("t1x", [64, T3], F32)
    t2 = P.sb("t2x", [64, T3], F32)
    load_fm(P, xh, xh_h, sIn)
    P.dma(P.qe, MK[:, :], mk_h.rearrange("p m c -> p (m c)"), sIn)
    P.dma(P.qe, ident[:, :], i128_h, sIn)
    P.dma(P.qe, cz[:, :], cz_h, sIn)
    P.dma(P.qe, sz[:, :], sz_h, sIn)
    P.inc(P.ve.memset(sc["ones1b"][:, :], 1.0), P.sem("ini"))
    P.inc(P.ve.memset(sc2["onesb64"][:, :], 1.0), P.sem("ini"))
    P.inc(P.ve.memset(sc2["ones1b"][:, :], 1.0), P.sem("ini"))
    for e in (P.te, P.ve, P.se):
        P.wait(e, sIn)
        P.wait(e, P.sem("ini"))
    ws = WStream(P, "w", wqkv_h, 12, NKC)
    sStq = P.sem("stq")

    def postq(mc, tt, bank, a0, a1, sTl, sEl):
        n = a1 - a0
        slot = mc % 2
        P.wait(P.se, sTl)
        if mc >= 2 and tt == 0:
            P.wait(P.se, sStq)
        P.inc(P.se.activation(ring.ch(slot, a0, a1), bank[:, 0:n], AF.Copy), sEl)
        P.wait(P.ge, sEl)
        P.dma(P.ge, qkv_d[mc * PB:(mc + 1) * PB, a0:a1], ring.ch(slot, a0, a1),
              sStq)
    emit_gemm(P, ws, xh, T3, sem_pref="gq", pool=pool, post=postq)
    P.barrier()
    sched = []
    for t in range(4):
        off6 = 6 if t == 0 else 0
        kbs = [(kb, ("m", kb - 4 * t - 2 + off6), 0, None)
               for kb in range(4 * t + 2, 4 * t + 8)]
        sched.append((t + 1, kbs))
    for h in range(8):
        for (dst, base) in ((QH, 0), (KH, 512)):
            feat = base + h * 64
            P.barrier()
            sHl = P.sem("ldq")
            P.dma(P.qe, QS[:, :], qkv_d[feat:feat + 64, :], sHl)
            P.wait(P.ve, sHl)
            srcap = QS[:, :]
            P.wait(P.ve, sS)
            P.inc(P.ve.tensor_scalar_mul(t2[0:32, :], srcap[32:64, :], -1.0), sV)
            P.inc(P.ve.tensor_copy(t2[32:64, :], srcap[0:32, :]), sV)
            P.inc(P.ve.tensor_copy(t1[:, :], srcap[:, :]), sV)
            P.wait(P.ve, sV)
            P.inc(P.ve.tensor_tensor(t1[:, :], t1[:, :], cz[:, :], ALU.mult), sV)
            P.inc(P.ve.tensor_tensor(t2[:, :], t2[:, :], sz[:, :], ALU.mult), sV)
            P.wait(P.ve, sV)
            P.inc(P.ve.tensor_tensor(t1[:, :], t1[:, :], t2[:, :], ALU.add), sV)
            P.wait(P.ve, sV)
            P.inc(P.ve.tensor_copy(dst[:, :], t1[:, :]), sV)
        for e in (P.te, P.se):
            P.wait(e, sV)
        emit_head_bound(P, QH[:, :], KH[:, :], T3, T3, cb[:, :], pool, sc2,
                        pref="hb")
        feat = 1024 + h * 64
        sHv = P.sem("ldv")
        P.dma(P.qe, VH[:, :], qkv_d[feat:feat + 64, :], sHv)
        for e in (P.te, P.ve):
            P.wait(e, sHv)
        emit_vaug(P, VH[:, :], VAW, 20, ident, pool, pref="tv")
        for e in (P.te, P.ve, P.se):
            P.wait(e, P.sem("tvv"))
        hd = {
            "Q": [QH[:, :]], "K": [KH[:, :]], "VA": VAW[:, :],
            "sched": [(qt, [(kb, MK[:, 512 * m[1]:512 * m[1] + 512], 0, None)
                            for (kb, m, _v, _mv) in kbs])
                      for (qt, kbs) in sched],
            "cb": cb[:, :], "nr": 64,
            "ctx": lambda a, b, h=h: OF.t[(h % 2) * 64:(h % 2) * 64 + 64,
                                          (h // 2) * TC + (a - 512):(h // 2) * TC + (b - 512)],
            "sc": sc,
        }
        emit_attn(P, [hd], T3, pool, pref="at")
    P.barrier()
    ws.w, ws.nmc, ws.nkc = wout_h, NKC, 4
    sSto = P.sem("sto")

    def posto(mc, tt, bank, a0, a1, sTl, sEl):
        n = a1 - a0
        slot = mc % 2
        P.wait(P.se, sTl)
        if mc >= 2 and tt == 0:
            P.wait(P.se, sSto)
        P.inc(P.se.activation(ring.ch(slot, a0, a1), bank[:, 0:n], AF.Copy), sEl)
        P.wait(P.ge, sEl)
        P.dma(P.ge, s_h[mc * PB:(mc + 1) * PB, a0:a1], ring.ch(slot, a0, a1),
              sSto)
    emit_gemm(P, ws, OF, TC, sem_pref="gwo", pool=pool, post=posto)
    P.end()
    return P


def stage_l3a(inputs, core, xh_fm):
    b, half = core // 2, core % 2
    own0 = half * TC
    Wqkv = np.asarray(inputs["L3_attn_Wqkv"], np.float32)
    Wout = np.asarray(inputs["L3_attn_Wout"], np.float32)
    HD = 64
    half_i = np.arange(0, HD, 2, dtype=np.float32)
    inv_freq = 1.0 / 10000.0 ** (half_i / HD)
    t = np.arange(T3, dtype=np.float32) + (own0 - 512)
    fr = t[:, None] * inv_freq[None, :]
    cos = np.cos(fr); sin = np.sin(fr)
    cz = np.concatenate([cos, cos], axis=1).T.copy()
    sz = np.concatenate([sin, sin], axis=1).T.copy()
    mk = np.zeros((6, PB, 512), np.float32)
    for o in range(6):
        ki = np.arange(PB)[:, None]
        qi = np.arange(512)[None, :]
        d = qi - ki + 512 - 128 * (o + 2)
        bad = (d < 0) | (d >= 256)
        mk[o][bad] = NEG
        if half == 0:
            kstage = ki + 128 * (o + 2)
    mks = []
    mk2 = mk.copy()
    if half == 0:
        for o in range(6):
            ki = np.arange(PB)[:, None] + 128 * (o + 2)
            mk2[o][(ki < 512) & np.ones((1, 512), bool)] = NEG
    mk_all = np.concatenate([mk, mk2], axis=0)
    return {"xh": xh_fm, "wqkv": pack_w(Wqkv),
            "cz": np.ascontiguousarray(cz), "sz": np.ascontiguousarray(sz),
            "mk": np.ascontiguousarray(mk_all.transpose(1, 0, 2)),
            "i128": np.eye(PB, dtype=BF),
            "wout": pack_w(Wout)}



def _run8(P, in_maps):
    res = run_bass_kernel_spmd(P.nc, in_maps, core_ids=list(range(NCORES)))
    return res.results


def kernel(_upto=99, _dbg=None, **inputs):
    cores = list(range(NCORES))
    dbg = _dbg if _dbg is not None else {}
    P = build_l0a()
    r1 = _run8(P, [stage_l0a(inputs, c) for c in cores])
    xn = [r1[c]["xn"] for c in cores]
    dbg["l0a"] = r1
    if _upto <= 0:
        return dbg
    P = build_l0b()
    r2 = _run8(P, [stage_l0b(inputs, c, r1) for c in cores])
    dbg["l0b"] = r2
    ctxf = []
    for c in cores:
        b, half = c // 2, c % 2
        cf = np.concatenate([r2[2 * b]["ctx"], r2[2 * b + 1]["ctx"]],
                            axis=0).reshape(D, S)
        ctxf.append(np.ascontiguousarray(cf[:, half * TC:(half + 1) * TC]))
    if _upto <= 1:
        return dbg
    P = build_l0c()
    maps = []
    for c in cores:
        m = stage_l0c(inputs, c, ctxf[c])
        m["xn"] = xn[c]
        maps.append(m)
    r3 = _run8(P, maps)
    dbg["l0c"] = r3
    xn_full = np.concatenate(xn, axis=1)
    lgt_tm = np.concatenate([np.asarray(r3[c]["lg"][0:8], np.float32).T
                             for c in cores], axis=0)
    if _upto <= 2:
        return dbg
    P = build_l0d()
    r4 = _run8(P, [stage_l0d(inputs, e, xn_full, lgt_tm) for e in cores])
    dbg["l0d"] = r4
    if _upto <= 3:
        return dbg
    P = build_l0e()
    maps = []
    for c in cores:
        pr = np.stack([r4[e]["part"][:, c * TC:(c + 1) * TC] for e in cores])
        maps.append(stage_l0e(inputs, c, r3[c]["s"], r3[c]["sh"], pr,
                              [stage_l0a(inputs, c)["x"] for _ in range(1)][0]))
    r5 = _run8(P, maps)
    dbg["l0e"] = r5
    if _upto <= 4:
        return dbg
    P = build_l1b()
    maps = []
    for c in cores:
        ktm = np.ascontiguousarray(
            np.asarray(r5[c]["kf"]).T.reshape(TC // PB, PB, D).transpose(1, 0, 2))
        vtm = np.ascontiguousarray(
            np.asarray(r5[c]["vf"]).T.reshape(TC // PB, PB, D).transpose(1, 0, 2))
        m = {"ktm": ktm, "vtm": vtm, "xn1": r5[c]["xn1"]}
        for nm, key in (("w1", "w1_w"), ("w2", "w2_w")):
            m[nm] = pack_w(np.asarray(inputs[f"L1_ffn_{key}"], np.float32).T)
        m["h1"] = pack_w(np.asarray(inputs["L1_ffn_h1_w"], np.float32).T)
        m["h2"] = pack_w(np.asarray(inputs["L1_ffn_h2_w"], np.float32).T)
        m["w3"] = pack_w(np.asarray(inputs["L1_ffn_w3_w"], np.float32).T)
        m["s1s"] = np.asarray(inputs["L1_ffn_w1_s"], np.float32).reshape(32, PB).T.copy()
        m["s2s"] = np.asarray(inputs["L1_ffn_w2_s"], np.float32).reshape(32, PB).T.copy()
        m["hs1"] = np.asarray(inputs["L1_ffn_h1_s"], np.float32).reshape(32, PB).T.copy()
        m["hs2"] = np.asarray(inputs["L1_ffn_h2_s"], np.float32).reshape(32, PB).T.copy()
        m["s3s"] = np.asarray(inputs["L1_ffn_w3_s"], np.float32).reshape(NKC, PB).T.copy()
        maps.append(m)
    r6 = _run8(P, maps)
    dbg["l1b"] = r6
    if _upto <= 5:
        return dbg
    P = build_l1c()
    maps = []
    for c in cores:
        b = c // 2
        kvp = np.ascontiguousarray(
            np.stack([np.asarray(r6[2 * b]["kvp"], np.float32),
                      np.asarray(r6[2 * b + 1]["kvp"], np.float32)], axis=1))
        maps.append({"kvp": kvp, "qf": r5[c]["qf"], "g1": r6[c]["g1"],
                     "x": r5[c]["xo"],
                     "wo": pack_w(np.asarray(inputs["L1_attn_Wo"], np.float32).T),
                     "bo": np.asarray(inputs["L1_attn_bo"], np.float32).reshape(NKC, PB).T.copy()})
    r7 = _run8(P, maps)
    dbg["l1c"] = r7
    if _upto <= 6:
        return dbg
    P = build_l1d()
    r8 = _run8(P, [stage_l1d(inputs, r7[c]["xo"]) for c in cores])
    dbg["l1d"] = r8
    if _upto <= 7:
        return dbg
    P = build_l2a()
    qkv_by_core = {c: np.asarray(r8[c]["qkv"]) for c in cores}
    r9 = _run8(P, [stage_l2a(inputs, c, qkv_by_core) for c in cores])
    dbg["l2a"] = r9
    ctxf2 = []
    for c in cores:
        b, half = c // 2, c % 2
        cf = np.concatenate([r9[2 * b]["ctx"], r9[2 * b + 1]["ctx"]],
                            axis=0).reshape(D, S)
        ctxf2.append(np.ascontiguousarray(cf[:, half * TC:(half + 1) * TC]))
    if _upto <= 8:
        return dbg
    P = build_l2b()
    maps = [{"ctx": ctxf2[c],
             "wo": pack_w(np.asarray(inputs["L2_attn_out_w"], np.float32).T),
             "bo": np.asarray(inputs["L2_attn_out_b"], np.float32).reshape(NKC, PB).T.copy()}
            for c in cores]
    r10 = _run8(P, maps)
    dbg["l2b"] = r10
    if _upto <= 9:
        return dbg

    def halo_stitch(outs, key, zeros_dtype=BF):
        res = []
        for c in cores:
            b, half = c // 2, c % 2
            a = np.asarray(outs[2 * b][key])
            bb = np.asarray(outs[2 * b + 1][key])
            full = np.concatenate([np.zeros((D, 1536), a.dtype), a, bb], axis=1)
            own0 = 1536 + half * TC
            res.append(np.ascontiguousarray(full[:, own0 - 1536: own0 + TC]))
        return res

    P_in = build_inner("p")
    xn2 = {c: np.asarray(r8[c]["xn"]) for c in cores}
    xh = halo_stitch(r8, "xn")
    r11 = _run8(P_in, [stage_inner(inputs, c, 0, "L2_ffn", xh[c]) for c in cores])
    dbg["l2c"] = r11
    if _upto <= 10:
        return dbg
    P_ic = build_inner("c", combine=True)
    xh2 = halo_stitch(r11, "xo")
    maps = []
    for c in cores:
        m = stage_inner(inputs, c, 1, "L2_ffn", xh2[c])
        m["xr"] = r7[c]["xo"]
        m["sr"] = r10[c]["s"]
        maps.append(m)
    r12 = _run8(P_ic, maps)
    dbg["l2d"] = r12
    if _upto <= 11:
        return dbg
    P = build_l3a()
    maps = []
    for c in cores:
        b, half = c // 2, c % 2
        a = np.asarray(r12[2 * b]["xnn"])
        bb = np.asarray(r12[2 * b + 1]["xnn"])
        full = np.concatenate([np.zeros((D, 512), a.dtype), a, bb], axis=1)
        own0 = 512 + half * TC
        xh3 = np.ascontiguousarray(full[:, own0 - 512: own0 + TC])
        maps.append(stage_l3a(inputs, c, xh3))
    r13 = _run8(P, maps)
    dbg["l3a"] = r13
    if _upto <= 12:
        return dbg
    xh = halo_stitch(r12, "xnn")
    r14 = _run8(P_in, [stage_inner(inputs, c, 0, "L3_ffn", xh[c]) for c in cores])
    dbg["l3c"] = r14
    if _upto <= 13:
        return dbg
    xh2 = halo_stitch(r14, "xo")
    maps = []
    for c in cores:
        m = stage_inner(inputs, c, 1, "L3_ffn", xh2[c])
        m["xr"] = r12[c]["xof"]
        m["sr"] = r13[c]["s"]
        maps.append(m)
    r15 = _run8(P_ic, maps)
    dbg["l3d"] = r15
    out = np.zeros((B, S, D), np.float32)
    for c in cores:
        b, half = c // 2, c % 2
        out[b, half * TC:(half + 1) * TC, :] = np.asarray(r15[c]["xof"], np.float32).T
    return out
